# version 02
# baseline (speedup 1.0000x reference)
import numpy as np
import ml_dtypes
import concourse.bass as bass
import concourse.mybir as mybir

F32 = mybir.dt.float32
BF16 = mybir.dt.bfloat16
AF = mybir.ActivationFunctionType
ALU = mybir.AluOpType
AX = mybir.AxisListType
bf = ml_dtypes.bfloat16

ENGINES = ("sync", "tensor", "vector", "scalar", "gpsimd")

D = 2048
INTER = 5632
NIC = INTER // 128
TOK = 2048
HALF = 1024
LN_EPS = 1e-5
RMS_EPS = 1e-6


class Prog:

    def __init__(self, nc, ctx):
        self.nc = nc
        self.ctx = ctx
        self.ops = {e: [] for e in ENGINES}
        self.sems = {}
        self.cnt = {}
        self.waits = []

    def sem(self, name):
        if name not in self.sems:
            self.sems[name] = self.ctx.enter_context(self.nc.semaphore(name))
            self.cnt[name] = 0
        return self.sems[name]

    def dma(self, eng, out, in_, sem=None, transpose=False):
        tgt = None
        h = None
        if sem is not None:
            h = self.sem(sem)
            self.cnt[sem] += 16
            tgt = self.cnt[sem]

        def _f(e, out=out, in_=in_, h=h, t=transpose):
            i = e.dma_start(out=out, in_=in_, transpose=True) if t else \
                e.dma_start(out=out, in_=in_)
            if h is not None:
                i.then_inc(h, 16)
        self.ops[eng].append(_f)
        return tgt

    def wait(self, eng, sem, val):
        if val is None or val <= 0:
            return
        h = self.sem(sem)
        self.waits.append((eng, sem, val, len(self.ops[eng])))
        self.ops[eng].append(lambda e, h=h, v=val: e.wait_ge(h, v))

    def op(self, eng, fn, sem=None, by=1):
        tgt = None
        h = None
        if sem is not None:
            h = self.sem(sem)
            self.cnt[sem] += by
            tgt = self.cnt[sem]

        def _f(e, fn=fn, h=h, by=by):
            i = fn(e)
            if h is not None:
                i.then_inc(h, by)
        self.ops[eng].append(_f)
        return tgt

    def check(self):
        bad = [(e, s_, v, i) for (e, s_, v, i) in self.waits if v > self.cnt[s_]]
        for b in bad[:20]:
            print("OVERWAIT:", b, "final=", self.cnt[b[1]])
        assert not bad, f"{len(bad)} over-waits"

    def run_block(self):
        self.check()
        with self.nc.Block() as block:
            for e in ENGINES:
                fns = self.ops[e]
                if not fns:
                    continue

                def body(eng, fns=fns):
                    for f in fns:
                        f(eng)
                getattr(block, e)(body)



def host_ffn_weights(gate_w, up_w, down_w, gate_s, up_s, down_s):
    WgT = np.ascontiguousarray(gate_w.T)
    WuT = np.ascontiguousarray(up_w.T)
    Wg_s = np.ascontiguousarray(WgT.reshape(D, NIC, 128).transpose(1, 0, 2))
    Wu_s = np.ascontiguousarray(WuT.reshape(D, NIC, 128).transpose(1, 0, 2))
    WdT = down_w.T
    Wd_s = np.ascontiguousarray(WdT.reshape(NIC, 128, 4, 512).transpose(2, 0, 1, 3))
    gs2 = np.ascontiguousarray(gate_s.reshape(NIC, 128).T)
    us2 = np.ascontiguousarray(up_s.reshape(NIC, 128).T)
    return dict(Wg_s=Wg_s, Wu_s=Wu_s, Wd_s=Wd_s, gs2=gs2, us2=us2,
                ds_row=np.ascontiguousarray(down_s.reshape(1, D)))


def host_consts():
    return dict(
        IDENT=np.eye(128, dtype=bf),
        IDENTF=np.eye(128, dtype=np.float32),
        ONESF=np.ones((1, 128), dtype=np.float32),
    )



class SharedBufs:

    def __init__(self, p):
        nc, e = p.nc, p.ctx.enter_context
        self.ident = e(nc.sbuf_tensor("ident", [128, 128], BF16))
        self.onesf = e(nc.sbuf_tensor("onesf", [1, 128], F32))
        self.kc = e(nc.sbuf_tensor("kconsts", [128, 8], F32))
        self.onesb = e(nc.sbuf_tensor("onesb", [128, 128], F32))
        self.flush = e(nc.sbuf_tensor("flush", [128, 1], F32))
        self.sbb = e(nc.sbuf_tensor("sbb", [128, 128], F32))

    def load_consts(self, p, ins):
        p.dma("sync", self.ident[:], ins["IDENT"], sem="c_ld")
        t = p.dma("sync", self.onesf[:], ins["ONESF"], sem="c_ld")
        p.op("vector", lambda e, o=self.kc: e.memset(o[:, 0:1], float(LN_EPS)))
        p.op("vector", lambda e, o=self.kc: e.memset(o[:, 1:2], -0.5))
        p.op("vector", lambda e, o=self.kc: e.memset(o[:, 2:3], float(RMS_EPS)))
        p.op("vector", lambda e, o=self.kc: e.memset(
            o[:, 3:4].bitcast(mybir.dt.int32), 0x5F3759DF))
        p.op("vector", lambda e, o=self.kc: e.memset(o[:, 4:5], -1.0))
        p.op("vector", lambda e, o=self.onesb: e.memset(o[:], 1.0))
        p.op("vector", lambda e, o=self.kc: e.memset(o[:, 5:6], 1.0), sem="c_kc")
        return t


def emit_ln_stats_tile(p, sb, lb, par, ld_val, tt):
    xt, xo, sbb = lb.xt[par], lb.xln_tm[par], sb.sbb
    p.wait("vector", f"lnx{par}", ld_val)
    p.op("vector", lambda e, a=xt, o=sbb, tt=tt: e.reduce_sum(
        o[:, 2 * tt:2 * tt + 1], a[:], axis=AX.X))
    p.op("vector", lambda e, a=xt, o=xo: e.tensor_tensor(o[:], a[:], a[:], ALU.mult))
    t = p.op("vector", lambda e, a=xo, o=sbb, tt=tt: e.reduce_sum(
        o[:, 2 * tt + 1:2 * tt + 2], a[:], axis=AX.X), sem="ln_st")
    return t


def _vc(p, fn):
    t = p.op("vector", fn, sem="vself")
    p.wait("vector", "vself", t)


def emit_rstd_batch(p, sb, nt):
    sbb = sb.sbb
    ev = sbb[:, 0:2 * nt:2]
    od = sbb[:, 1:2 * nt:2]
    mu = sbb[:, 32:32 + nt]
    rs = sbb[:, 48:48 + nt]
    s1 = sbb[:, 64:64 + nt]
    s2 = sbb[:, 80:80 + nt]
    if p.cnt.get("c_kc", 0) and p.cnt.get("vself", 0) == 0:
        p.wait("vector", "c_kc", p.cnt["c_kc"])
    if p.cnt.get("ln_st", 0) > 0:
        p.wait("vector", "ln_st", p.cnt["ln_st"])
    _vc(p, lambda e: e.tensor_scalar(mu, ev, 1.0 / D, None, ALU.mult))
    _vc(p, lambda e: e.tensor_scalar(s1, od, 1.0 / D, None, ALU.mult))
    _vc(p, lambda e: e.tensor_tensor(s2, mu, mu, ALU.mult))
    _vc(p, lambda e: e.tensor_tensor(s1, s1, s2, ALU.subtract))
    _vc(p, lambda e: e.tensor_scalar(s1, s1, float(LN_EPS), None, ALU.add))
    I32 = mybir.dt.int32
    _vc(p, lambda e: e.tensor_scalar(
        s2.bitcast(I32), s1.bitcast(I32), 1, None, ALU.logical_shift_right))
    _vc(p, lambda e: e.tensor_scalar(
        s2.bitcast(I32), s2.bitcast(I32), 0, None, ALU.bitwise_not))
    _vc(p, lambda e: e.tensor_scalar(
        rs.bitcast(I32), s2.bitcast(I32), 0x5F3759DF + 1, None, ALU.add))
    for _ in range(3):
        _vc(p, lambda e: e.tensor_tensor(s2, rs, rs, ALU.mult))
        _vc(p, lambda e: e.tensor_tensor(s2, s2, s1, ALU.mult))
        _vc(p, lambda e: e.tensor_scalar(s2, s2, -0.5, 1.5, ALU.mult, ALU.add))
        _vc(p, lambda e: e.tensor_tensor(rs, rs, s2, ALU.mult))
    return p.cnt["vself"]


def emit_ln_norm_tile(p, sb, lb, par, ld_val, tt):
    xt, xo, sbb = lb.xt[par], lb.xln_tm[par], sb.sbb
    p.wait("vector", f"lnx{par}", ld_val)
    t_ln = p.op("vector", lambda e, a=xt, o=xo, sbb=sbb, tt=tt: e.tensor_scalar(
        o[:], a[:], sbb[:, 32 + tt:33 + tt], sbb[:, 48 + tt:49 + tt],
        ALU.subtract, ALU.mult), sem="ln_out")
    return t_ln



ARENA_F32 = 52224


class Arena:

    def __init__(self, p):
        self.t = p.ctx.enter_context(
            p.nc.sbuf_tensor("arena", [128, ARENA_F32], F32))
        self.off = 0

    def f32(self, n):
        self.off = (self.off + 7) & ~7
        v = self.t[:, self.off:self.off + n]
        self.off += n
        return v

    def bf16(self, n):
        assert n % 2 == 0
        self.off = (self.off + 7) & ~7
        v = self.t[:, self.off:self.off + n // 2].bitcast(BF16)
        self.off += n // 2
        return v

    def reset(self):
        self.off = 0


class PsumBufs:
    def __init__(self, p):
        e = p.ctx.enter_context
        self.ps1k = [e(p.nc.psum_tensor(f"ps1k{i}", [128, 1024], F32))
                     for i in range(4)]
        self.ps = [self.ps1k[i // 2][:, (i % 2) * 512:(i % 2) * 512 + 512]
                   for i in range(8)]



class FFNBufs:
    def __init__(self, p, ar, pb):
        nc, e = p.nc, p.ctx.enter_context
        ar.reset()
        self.xln_fm = ar.bf16(16 * HALF)
        self.hdn = ar.bf16(NIC * HALF)
        self.wg = [ar.bf16(16 * 128) for i in range(2)]
        self.wu = [ar.bf16(16 * 128) for i in range(2)]
        self.wdq = [ar.bf16(10 * 512) for i in range(2)]
        self.sg = [ar.bf16(HALF) for i in range(2)]
        self.xt = [ar.f32(D) for i in range(2)]
        self.xln_tm = [ar.bf16(D) for i in range(2)]
        self.dsb = ar.f32(D)
        self.xs = [ar.f32(512) for i in range(2)]
        self.res = [ar.f32(512) for i in range(2)]
        self.ev = [ar.f32(512) for i in range(2)]
        self.gs = e(nc.sbuf_tensor("gs", [128, NIC], F32))
        self.us = e(nc.sbuf_tensor("us", [128, NIC], F32))
        self.ps1k = pb.ps1k
        self.ps = pb.ps


def emit_ffn_consts(p, fb, sb, ins):
    p.dma("sync", fb.gs[:], ins["gs2"], sem="c_ld")
    p.dma("sync", fb.us[:], ins["us2"], sem="c_ld")
    dsrow = fb.xt[0][0:1, :]
    t = p.dma("sync", dsrow, ins["ds_row"], sem="c_ld")
    p.wait("tensor", "c_ld", t)
    for j in range(4):
        last = (j == 3)
        p.op("tensor", lambda e, fb=fb, sb=sb, dsrow=dsrow, j=j: e.matmul(
            fb.ps[j][:, :], sb.onesf[:], dsrow[:, j * 512:(j + 1) * 512],
            start=True, stop=True), sem="t_dsb" if last else None)
    p.wait("vector", "t_dsb", p.cnt["t_dsb"])
    for j in range(4):
        t = p.op("vector", lambda e, fb=fb, j=j: e.tensor_scalar(
            fb.dsb[:, j * 512:(j + 1) * 512], fb.ps[j][:, :], 0.5, None, ALU.mult),
            sem="v_dsb")
    return t


def emit_ffn_pass(p, fb, sb, src_x, dst_x1, W, tag, dbg=None):
    s_ldw = "ffn_ldw"
    s_tgu = "ffn_tgu"
    s_ssg = "ffn_ssg"
    s_vhd = "ffn_vhd"
    s_ldd = "ffn_ldd"
    s_tdn = "ffn_tdn"
    s_ldxs = "ffn_ldxs"
    s_vdn = "ffn_vdn"
    s_stx = "ffn_stx"
    s_ttr = "ffn_ttr"
    s_vtr = "ffn_vtr"
    for s in (s_ldw, s_tgu, s_ssg, s_vhd, s_ldd, s_tdn, s_ldxs, s_vdn, s_stx,
              s_ttr, s_vtr, "ln_out"):
        p.sem(s)

    if "t_dsb" in p.cnt:
        p.wait("sync", "t_dsb", p.cnt["t_dsb"])
    p.sem(f"{s_stx}0")
    p.sem(f"{s_stx}1")
    for h in range(2):
        if p.cnt[s_tgu] > 0:
            p.wait("vector", s_tgu, p.cnt[s_tgu])
        if p.cnt[s_vdn] > 0:
            p.wait("tensor", s_vdn, p.cnt[s_vdn])
        if "v_dsb" in p.cnt and p.cnt[s_ttr] == 0:
            p.wait("tensor", "v_dsb", p.cnt["v_dsb"])
        p.sem("ln_st")
        if p.cnt[s_ttr] > 0:
            p.wait("vector", s_ttr, p.cnt[s_ttr])
        for tt in range(8):
            par = tt % 2
            row0 = h * HALF + tt * 128
            if p.cnt["ln_st"] >= 2:
                p.wait("sync", "ln_st", p.cnt["ln_st"] - 1)
            if tt < 2 and p.cnt["ln_out"] >= 2:
                p.wait("sync", "ln_out", p.cnt["ln_out"] - 1)
            lv = p.dma("sync", fb.xt[par][:], src_x[row0:row0 + 128, :],
                       sem=f"lnx{par}")
            emit_ln_stats_tile(p, sb, fb, par, lv, tt)
        emit_rstd_batch(p, sb, 8)
        for tt in range(8):
            par = tt % 2
            row0 = h * HALF + tt * 128
            if tt < 2:
                p.wait("sync", "ln_st", p.cnt["ln_st"])
            if p.cnt["ln_out"] >= 2:
                p.wait("sync", "ln_out", p.cnt["ln_out"] - 1)
            if p.cnt[s_vtr] >= 8:
                p.wait("sync", s_vtr, p.cnt[s_vtr] - 4)
            lv = p.dma("sync", fb.xt[par][:], src_x[row0:row0 + 128, :],
                       sem=f"lnx{par}")
            t_ln = emit_ln_norm_tile(p, sb, fb, par, lv, tt)
            p.wait("tensor", "ln_out", t_ln)
            if p.cnt[s_vtr] > 0:
                p.wait("tensor", s_vtr, p.cnt[s_vtr])
            for hc in range(16):
                bank, slot = hc // 4, hc % 4
                p.op("tensor", lambda e, fb=fb, sb=sb, par=par, hc=hc, bank=bank,
                     slot=slot: e.matmul(
                         fb.ps[bank][:, slot * 128:slot * 128 + 128],
                         fb.xln_tm[par][:, hc * 128:hc * 128 + 128],
                         sb.ident[:], start=True, stop=True),
                     sem=s_ttr if hc == 15 else None)
            p.wait("vector", s_ttr, p.cnt[s_ttr])
            for bank in range(4):
                hc0 = bank * 4
                dst = fb.xln_fm[:].rearrange("p (hc t) -> p hc t", hc=16)[
                    :, hc0:hc0 + 4, tt * 128:tt * 128 + 128]
                src = fb.ps[bank][:].rearrange("p (s t) -> p s t", s=4)
                p.op("vector", lambda e, dst=dst, src=src: e.tensor_copy(dst, src),
                     sem=s_vtr)

        for ic in range(44):
            par = ic % 2
            first2 = (h == 0 and ic < 2)
            if not first2:
                p.wait("gpsimd", s_tgu, p.cnt[s_tgu] - 1)
            p.dma("gpsimd", fb.wg[par][:].rearrange("p (a b) -> p a b", a=16),
                  W["Wg_s"][ic].rearrange("(a p) i -> p a i", p=128),
                  sem=f"{s_ldw}{par}")
            lw = p.dma("gpsimd", fb.wu[par][:].rearrange("p (a b) -> p a b", a=16),
                       W["Wu_s"][ic].rearrange("(a p) i -> p a i", p=128),
                       sem=f"{s_ldw}{par}")
            p.wait("tensor", f"{s_ldw}{par}", lw)
            if ic == 0:
                p.wait("tensor", s_vtr, p.cnt[s_vtr])
            if not first2:
                p.wait("tensor", s_vhd, p.cnt[s_vhd] - 1)
            for w_i, (wt, pb) in enumerate(((fb.wg[par], fb.ps1k[par * 2]),
                                            (fb.wu[par], fb.ps1k[par * 2 + 1]))):
                for tg in range(2):
                    for hc in range(16):
                        last = (w_i == 1 and tg == 1 and hc == 15)
                        p.op("tensor", lambda e, fb=fb, wt=wt, pb=pb, tg=tg, hc=hc:
                             e.matmul(
                                 pb[:, tg * 512:tg * 512 + 512],
                                 wt[:, hc * 128:hc * 128 + 128],
                                 fb.xln_fm[:, hc * HALF + tg * 512:
                                           hc * HALF + tg * 512 + 512],
                                 start=(hc == 0), stop=(hc == 15)),
                             sem=s_tgu if last else None)
            t_gu = p.cnt[s_tgu]
            p.wait("scalar", s_tgu, t_gu)
            p.op("scalar", lambda e, fb=fb, par=par, ic=ic: e.activation(
                fb.sg[par][:], fb.ps1k[par * 2][:], AF.Silu,
                scale=fb.gs[:, ic:ic + 1]))
            t_sg = p.op("scalar", lambda e, fb=fb, par=par, sb=sb: e.copy(
                sb.flush[:], fb.sg[par][:, HALF - 1:HALF]), sem=s_ssg)
            p.wait("vector", s_ssg, t_sg)
            p.op("vector", lambda e, fb=fb, par=par, ic=ic: e.scalar_tensor_tensor(
                fb.hdn[:, ic * HALF:(ic + 1) * HALF], fb.ps1k[par * 2 + 1][:],
                fb.us[:, ic:ic + 1], fb.sg[par][:], ALU.mult, ALU.mult), sem=s_vhd)
        vhd_half = p.cnt[s_vhd]
        if dbg is not None and h == 0:
            p.wait("sync", s_vhd, vhd_half)
            p.dma("sync", dbg["hdn0"], fb.hdn[:], sem="dbg0")
            p.dma("sync", dbg["xfm0"], fb.xln_fm[:], sem="dbg0")

        G_ICS = [(0, 10), (10, 10), (20, 10), (30, 10), (40, 4)]
        for hq in range(4):
            p.wait("tensor", s_vhd, vhd_half)
            if p.cnt[s_vdn] > 0:
                p.wait("tensor", s_vdn, p.cnt[s_vdn])
            for grp, (g0, gn) in enumerate(G_ICS):
                par = (hq * len(G_ICS) + grp) % 2
                if p.cnt[s_tdn] >= 2:
                    p.wait("gpsimd", s_tdn, p.cnt[s_tdn] - 1)
                ld = p.dma("gpsimd",
                           fb.wdq[par][:, 0:gn * 512].rearrange(
                               "p (a b) -> p a b", a=gn),
                           W["Wd_s"][hq, g0:g0 + gn].rearrange(
                               "a p i -> p a i"), sem=f"{s_ldd}{par}")
                p.wait("tensor", f"{s_ldd}{par}", ld)
                for ic_l in range(gn):
                    ic = g0 + ic_l
                    for tt in range(8):
                        last = (ic_l == gn - 1 and tt == 7)
                        p.op("tensor", lambda e, fb=fb, ic=ic, ic_l=ic_l, tt=tt,
                             par=par: e.matmul(
                                 fb.ps[tt][:],
                                 fb.hdn[:, ic * HALF + tt * 128:
                                        ic * HALF + tt * 128 + 128],
                                 fb.wdq[par][:, ic_l * 512:ic_l * 512 + 512],
                                 start=(ic == 0), stop=(ic == NIC - 1)),
                             sem=s_tdn if last else None)
            t_dn = p.cnt[s_tdn]
            for tt in range(8):
                bpar = tt % 2
                row0 = h * HALF + tt * 128
                if p.cnt[s_vdn] >= 2:
                    p.wait("sync", s_vdn, p.cnt[s_vdn] - 1)
                lx = p.dma("sync", fb.xs[bpar][:],
                           src_x[row0:row0 + 128, hq * 512:hq * 512 + 512],
                           sem=f"{s_ldxs}{bpar}")
                p.wait("vector", s_tdn, t_dn)
                p.wait("vector", f"{s_ldxs}{bpar}", lx)
                p.op("vector", lambda e, fb=fb, tt=tt, hq=hq, bpar=bpar:
                     e.tensor_tensor(fb.ev[bpar][:], fb.ps[tt][:],
                                     fb.dsb[:, hq * 512:hq * 512 + 512], ALU.mult))
                if p.cnt.get(f"{s_stx}{bpar}", 0) > 0:
                    p.wait("vector", f"{s_stx}{bpar}", p.cnt[f"{s_stx}{bpar}"])
                t_vd = p.op("vector", lambda e, fb=fb, bpar=bpar: e.tensor_tensor(
                    fb.res[bpar][:], fb.ev[bpar][:], fb.xs[bpar][:], ALU.add),
                    sem=s_vdn)
                p.wait("sync", s_vdn, t_vd)
                p.dma("sync", dst_x1[row0:row0 + 128, hq * 512:hq * 512 + 512],
                      fb.res[bpar][:], sem=f"{s_stx}{bpar}")
    return p.cnt[f"{s_stx}0"] + p.cnt[f"{s_stx}1"]


def emit_ln_store_pass(p, sb, fb, src_ap, dst_bf16_ap, nrows=TOK):
    s_st = "lnp_st"
    p.sem(s_st)
    p.sem("ln_st")
    nt = nrows // 128
    assert nt <= 16
    if p.cnt.get("ffn_ttr", 0) > 0:
        p.wait("vector", "ffn_ttr", p.cnt["ffn_ttr"])
    for tt in range(nt):
        par = tt % 2
        if p.cnt["ln_st"] >= 2:
            p.wait("sync", "ln_st", p.cnt["ln_st"] - 1)
        if tt < 2 and p.cnt["ln_out"] >= 2:
            p.wait("sync", "ln_out", p.cnt["ln_out"] - 1)
        lv = p.dma("sync", fb.xt[par][:], src_ap[tt * 128:tt * 128 + 128, :],
                   sem=f"lnx{par}")
        emit_ln_stats_tile(p, sb, fb, par, lv, tt)
    emit_rstd_batch(p, sb, nt)
    for tt in range(nt):
        par = tt % 2
        if tt < 2:
            p.wait("sync", "ln_st", p.cnt["ln_st"])
        if p.cnt["ln_out"] >= 2:
            p.wait("sync", "ln_out", p.cnt["ln_out"] - 1)
        if p.cnt.get(f"{s_st}{par}", 0) > 0:
            p.wait("sync", f"{s_st}{par}", p.cnt[f"{s_st}{par}"])
        lv = p.dma("sync", fb.xt[par][:], src_ap[tt * 128:tt * 128 + 128, :],
                   sem=f"lnx{par}")
        t_ln = emit_ln_norm_tile(p, sb, fb, par, lv, tt)
        p.wait("sync", "ln_out", t_ln)
        p.dma("sync", dst_bf16_ap[tt * 128:tt * 128 + 128, :], fb.xln_tm[par][:],
              sem=f"{s_st}{par}")
    return p.cnt[f"{s_st}0"] + p.cnt[f"{s_st}1"]



def np_ln(x):
    m = x.mean(-1, keepdims=True)
    v = x.var(-1, keepdims=True)
    return (x - m) / np.sqrt(v + LN_EPS)


def np_silu(x):
    return x / (1 + np.exp(-x))


def np_ffn(x, gate_w, up_w, down_w, gate_s, up_s, down_s):
    g = x @ gate_w.T * gate_s
    u = x @ up_w.T * up_s
    h = np_silu(g) * u
    return (h @ down_w.T) * down_s



NKH_C = 4
NVH_C = 8
CHUNK = 64
BLK = 512
NBLK = SEQ_BLKS = 16


def np_attn_full(x_ln, W_qkvz, W_ba, conv_kernel, A_log, dt_bias, norm_weight):
    S = x_ln.shape[0]
    NKH, NVH, DK, DV, KSZ = 16, 32, 128, 128, 4
    qkvz = x_ln @ W_qkvz
    ba = x_ln @ W_ba
    mixed = qkvz.reshape(S, NKH, 768)
    q = mixed[:, :, 0:128]
    k = mixed[:, :, 128:256]
    v = mixed[:, :, 256:512].reshape(S, NVH, DV)
    z = mixed[:, :, 512:768].reshape(S, NVH, DV)
    mba = ba.reshape(S, NKH, 4)
    b_g = mba[:, :, 0:2].reshape(S, NVH)
    a_g = mba[:, :, 2:4].reshape(S, NVH)
    qkv = np.concatenate([q.reshape(S, -1), k.reshape(S, -1), v.reshape(S, -1)], -1)
    x = np.concatenate([np.zeros((KSZ - 1, qkv.shape[1]), qkv.dtype), qkv], 0)
    conv = np.zeros_like(qkv)
    for j in range(KSZ):
        conv += x[j:j + S, :] * conv_kernel[j, 0, :][None, :]
    qkv_c = np_silu(conv)
    q_c = qkv_c[:, 0:NKH * DK].reshape(S, NKH, DK)
    k_c = qkv_c[:, NKH * DK:2 * NKH * DK].reshape(S, NKH, DK)
    v_c = qkv_c[:, 2 * NKH * DK:].reshape(S, NVH, DV)
    beta = 1 / (1 + np.exp(-b_g))
    sp = np.logaddexp(0, a_g + dt_bias[None, :])
    g = -np.exp(A_log)[None, :] * sp
    qq = np.repeat(q_c, 2, axis=1).astype(np.float64)
    kk = np.repeat(k_c, 2, axis=1).astype(np.float64)
    vv = v_c.astype(np.float64)
    gg = g.astype(np.float64)
    bb = beta.astype(np.float64)
    nc_ = S // CHUNK
    out = np.zeros((S, NVH, DV))
    for h in range(NVH):
        state = np.zeros((DK, DV))
        for c in range(nc_):
            sl = slice(c * CHUNK, (c + 1) * CHUNK)
            qi = qq[sl, h, :] / np.sqrt(DK)
            ki = kk[sl, h, :]
            vi = vv[sl, h, :]
            gi = gg[sl, h]
            bi = bb[sl, h]
            gc = np.cumsum(gi)
            decay = np.exp(np.tril(gc[:, None] - gc[None, :]))
            kbc = ki * bi[:, None]
            L = -(kbc @ ki.T) * decay
            L = np.tril(L, -1)
            attn = np.linalg.inv(np.eye(CHUNK) - L)
            v_beta = vi * bi[:, None]
            v_i = attn @ v_beta
            kcd = attn @ (kbc * np.exp(gc)[:, None])
            a_i = np.tril((qi @ ki.T) * decay)
            v_new = v_i - kcd @ state
            o_i = (qi * np.exp(gc)[:, None]) @ state + a_i @ v_new
            g_last = gc[-1]
            state = state * np.exp(g_last) + \
                (ki * np.exp(g_last - gc)[:, None]).T @ v_new
            out[sl, h, :] = o_i
    xf = out * np_silu(z.astype(np.float64))
    var = (xf * xf).mean(-1, keepdims=True)
    xf = xf * (1 / np.sqrt(var + RMS_EPS)) * norm_weight[None, None, :]
    return xf.reshape(S, NVH * DV).astype(np.float32)


def np_attn_full_parts(x_ln, W_qkvz, W_ba, conv_kernel, A_log, dt_bias,
                       norm_weight):
    S = x_ln.shape[0]
    NKH, NVH, DK, DV, KSZ = 16, 32, 128, 128, 4
    qkvz = x_ln @ W_qkvz
    ba = x_ln @ W_ba
    mixed = qkvz.reshape(S, NKH, 768)
    q = mixed[:, :, 0:128]
    k = mixed[:, :, 128:256]
    v = mixed[:, :, 256:512].reshape(S, NVH, DV)
    z = mixed[:, :, 512:768].reshape(S, NVH, DV)
    mba = ba.reshape(S, NKH, 4)
    b_g = mba[:, :, 0:2].reshape(S, NVH)
    a_g = mba[:, :, 2:4].reshape(S, NVH)
    qkv = np.concatenate([q.reshape(S, -1), k.reshape(S, -1), v.reshape(S, -1)], -1)
    x = np.concatenate([np.zeros((KSZ - 1, qkv.shape[1]), qkv.dtype), qkv], 0)
    conv = np.zeros_like(qkv)
    for j in range(KSZ):
        conv += x[j:j + S, :] * conv_kernel[j, 0, :][None, :]
    qkv_c = np_silu(conv)
    q_c = qkv_c[:, 0:NKH * DK].reshape(S, NKH, DK)
    k_c = qkv_c[:, NKH * DK:2 * NKH * DK].reshape(S, NKH, DK)
    v_c = qkv_c[:, 2 * NKH * DK:].reshape(S, NVH, DV)
    beta = 1 / (1 + np.exp(-b_g))
    sp = np.logaddexp(0, a_g + dt_bias[None, :])
    g = -np.exp(A_log)[None, :] * sp
    qq = np.repeat(q_c, 2, axis=1).astype(np.float64)
    kk = np.repeat(k_c, 2, axis=1).astype(np.float64)
    vv = v_c.astype(np.float64)
    gg = g.astype(np.float64)
    bb = beta.astype(np.float64)
    nc_ = S // CHUNK
    out = np.zeros((S, NVH, DV))
    states_at = {}
    for h in range(NVH):
        state = np.zeros((DK, DV))
        for c in range(nc_):
            sl = slice(c * CHUNK, (c + 1) * CHUNK)
            qi = qq[sl, h, :] / np.sqrt(DK)
            ki = kk[sl, h, :]
            vi = vv[sl, h, :]
            gi = gg[sl, h]
            bi = bb[sl, h]
            gc = np.cumsum(gi)
            decay = np.exp(np.tril(gc[:, None] - gc[None, :]))
            kbc = ki * bi[:, None]
            L = -(kbc @ ki.T) * decay
            L = np.tril(L, -1)
            attn = np.linalg.inv(np.eye(CHUNK) - L)
            v_beta = vi * bi[:, None]
            v_i = attn @ v_beta
            kcd = attn @ (kbc * np.exp(gc)[:, None])
            a_i = np.tril((qi @ ki.T) * decay)
            v_new = v_i - kcd @ state
            o_i = (qi * np.exp(gc)[:, None]) @ state + a_i @ v_new
            g_last = gc[-1]
            state = state * np.exp(g_last) + \
                (ki * np.exp(g_last - gc)[:, None]).T @ v_new
            out[sl, h, :] = o_i
            end = (c + 1) * CHUNK
            if end % 1024 == 0:
                states_at.setdefault(end, {})[h] = state.copy()
    return dict(q_c=q_c, k_c=k_c, v_c=v_c, z=z, g=g, beta=beta, out=out,
                states_at=states_at)


def host_attn_weights(core, W_qkvz, W_ba, conv_kernel, A_log, dt_bias):
    kh0 = (core % 4) * 4
    khs = list(range(kh0, kh0 + 4))
    vhs = [2 * k + r for k in khs for r in range(2)]
    qcols = [768 * k + d for k in khs for d in range(128)]
    kcols = [768 * k + 128 + d for k in khs for d in range(128)]
    vcols = [768 * k + 256 + r * 128 + d for k in khs for r in range(2)
             for d in range(128)]
    zcols = [768 * k + 512 + r * 128 + d for k in khs for r in range(2)
             for d in range(128)]
    bacols = [4 * k + r for k in khs for r in range(2)] + \
             [4 * k + 2 + r for k in khs for r in range(2)]
    Wqkv = W_qkvz[:, qcols + kcols + vcols]
    Wqkv_s = np.ascontiguousarray(Wqkv.reshape(D, 16, 128).transpose(1, 0, 2))
    Wz_s = np.ascontiguousarray(W_qkvz[:, zcols].reshape(16, 128, len(zcols)))
    Wba_c = np.ascontiguousarray(W_ba[:, bacols])
    chans = ([k * 128 + d for k in khs for d in range(128)] +
             [2048 + k * 128 + d for k in khs for d in range(128)] +
             [4096 + vh * 128 + d for vh in vhs for d in range(128)])
    convw = np.ascontiguousarray(
        conv_kernel[:, 0, chans].T.reshape(16, 128, 4))
    return dict(Wqkv_s=Wqkv_s, Wz_s=Wz_s, Wba_c=Wba_c, convw=convw,
                A_c=np.ascontiguousarray(A_log[vhs].reshape(1, 8)),
                dt_c=np.ascontiguousarray(dt_bias[vhs].reshape(1, 8)))


def host_attn_consts():
    sel = np.zeros((16, 16, 128), np.float32)
    for i in range(16):
        sel[i, i, :] = 1.0
    sel64 = np.zeros((128, 128), np.float32)
    sel64[63, 0:64] = 1.0
    sel64[127, 64:128] = 1.0
    u1 = np.zeros((128, 128), np.float32)
    for b0 in (0, 64):
        for i in range(64):
            u1[b0:b0 + i + 1, b0 + i] = 1.0
    low_s = np.zeros((128, 128), np.float32)
    up_s = np.zeros((128, 128), np.float32)
    up_i = np.zeros((128, 128), np.float32)
    for b0 in (0, 64):
        for i in range(64):
            for j in range(64):
                ii, jj = b0 + i, b0 + j
                if i > j:
                    low_s[ii, jj] = 1.0
                    up_s[jj, ii] = 1.0
                if i >= j:
                    up_i[jj, ii] = 1.0 / np.sqrt(128.0)
    t4 = lambda m: np.ascontiguousarray(np.tile(m, (1, 4)))
    return dict(SEL=sel, SEL64=sel64, U1BD=u1,
                MLOW4=t4(low_s).astype(bf), MUPS4=t4(up_s).astype(bf),
                MUPI4=t4(up_i).astype(bf))


class AttnBufs:

    def __init__(self, p, ar, pb):
        nc, e = p.nc, p.ctx.enter_context
        ar.reset()
        self.q_dm = ar.bf16(4 * 1024)
        self.k_dm = ar.bf16(4 * 1024)
        self.v_dm = ar.bf16(8 * 1024)
        self.z_act = ar.bf16(8 * 1024)
        self.outb = ar.bf16(8 * 1024)
        self.state = ar.f32(8 * 128)
        self.state_bf = ar.bf16(1024)
        self.tab = ar.f32(768)
        self.elb = ar.f32(8)
        self.epiv = ar.f32(64)
        self.wba = ar.bf16(16 * 16)
        self.tailb = ar.bf16(48)
        self.nwbc = ar.f32(128)
        self.negA = ar.f32(8)
        self.dtb = ar.f32(8)
        self.cdiag = ar.bf16(64 * 128)
        a0 = ar.off
        self.x_fm = ar.bf16(16 * 1024)
        self.wq = [ar.bf16(16 * 128) for _ in range(2)]
        self.wz = [ar.bf16(1024) for _ in range(2)]
        self.raw = ar.bf16(16 * 1032)
        proj_end = ar.off
        ar.off = a0
        self.E1m = ar.f32(1024)
        self.rec = ar.f32(1024)
        self.D1 = ar.bf16(1024)
        self.D1T = ar.bf16(1024)
        self.D2 = ar.bf16(1024)
        self.Dst = ar.bf16(1024)
        self.A = [ar.bf16(1024) for _ in range(6)]
        self.Ppp = [ar.bf16(1024) for _ in range(2)]
        self.Qpp = [ar.bf16(1024) for _ in range(2)]
        self.aT = ar.bf16(1024)
        self.kTM = ar.bf16(512)
        self.vbc = ar.bf16(1024)
        self.kmod = ar.bf16(1024)
        self.u = [ar.bf16(1024) for _ in range(2)]
        self.vnew = ar.bf16(1024)
        self.tsb = ar.f32(1024)
        delta_end = ar.off
        ar.off = max(proj_end, delta_end)
        self.rows = ar.f32(128)[0:16, :]
        self.sel = ar.f32(2048)[0:16, :]
        self.sel64 = ar.f32(128)
        self.u1bd = ar.f32(128)
        self.mlow4 = ar.bf16(512)
        self.mups4 = ar.bf16(512)
        self.mupi4 = ar.bf16(512)
        self.identf = ar.f32(128)
        self.ps1k = pb.ps1k
        self.ps = pb.ps


def emit_attn_consts(p, ab, sb, ins):
    cl = "ac_ld"
    tg_ = p.dma("gpsimd", ab.wba[:].rearrange("p (a b) -> p a b", a=16),
                ins["Wba_c"].rearrange("(a p) i -> p a i", p=128), sem="ac_gld")
    cw = ab.tsb[:, 0:64]
    p.dma("sync", cw.rearrange("p (a b) -> p a b", a=16),
          ins["convw"].rearrange("a p b -> p a b"), sem=cl)
    p.dma("sync", ab.negA[0:1, :], ins["A_c"], sem=cl)
    p.dma("sync", ab.dtb[0:1, :], ins["dt_c"], sem=cl)
    t = p.dma("sync", ab.nwbc[0:1, 0:128],
              ins["norm_weight"].rearrange("(a b) -> a b", a=1), sem=cl)
    p.dma("sync", ab.sel.rearrange("p (a b) -> p a b", a=16),
          ins["SEL"].rearrange("a p b -> p a b"), sem=cl)
    p.dma("sync", ab.sel64, ins["SEL64"], sem=cl)
    p.dma("sync", ab.u1bd, ins["U1BD"], sem=cl)
    p.dma("sync", ab.mlow4, ins["MLOW4"], sem=cl)
    p.dma("sync", ab.mups4, ins["MUPS4"], sem=cl)
    p.dma("sync", ab.mupi4, ins["MUPI4"], sem=cl)
    t = p.dma("sync", ab.identf, ins["IDENTF"], sem=cl)
    p.wait("tensor", cl, t)
    p.wait("tensor", "ac_gld", tg_)
    p.wait("tensor", "c_ld", p.cnt["c_ld"])
    p.wait("vector", "c_ld", p.cnt["c_ld"])
    p.wait("scalar", cl, t)
    t_e = p.op("scalar", lambda e: e.activation(ab.negA[0:1, :], ab.negA[0:1, :],
                                                AF.Exp), sem="ac_s")
    p.wait("vector", "ac_s", t_e)
    t_n = p.op("vector", lambda e: e.tensor_scalar(
        ab.negA[0:1, :], ab.negA[0:1, :], -1.0, None, ALU.mult), sem="ac_v")
    p.wait("tensor", "ac_v", t_n)
    p.op("tensor", lambda e: e.matmul(ab.ps[0][:, 0:8], sb.onesf[:],
                                      ab.negA[0:1, :], start=True, stop=True))
    p.op("tensor", lambda e: e.matmul(ab.ps[0][:, 8:16], sb.onesf[:],
                                      ab.dtb[0:1, :], start=True, stop=True))
    t_b = p.op("tensor", lambda e: e.matmul(ab.ps[1][:, 0:128], sb.onesf[:],
                                            ab.nwbc[0:1, 0:128],
                                            start=True, stop=True), sem="ac_t")
    p.wait("vector", "ac_t", t_b)
    p.op("vector", lambda e: e.tensor_copy(ab.negA[:], ab.ps[0][:, 0:8]))
    p.op("vector", lambda e: e.tensor_copy(ab.dtb[:], ab.ps[0][:, 8:16]))
    t_c = p.op("vector", lambda e: e.tensor_copy(ab.nwbc[:], ab.ps[1][:, 0:128]),
               sem="ac_v")
    p.wait("vector", cl, p.cnt[cl])
    for cb in range(16):
        for j in range(4):
            t_c = p.op("vector", lambda e, cb=cb, j=j: e.tensor_tensor(
                ab.cdiag[:, (cb * 4 + j) * 128:(cb * 4 + j) * 128 + 128],
                sb.ident[:],
                cw[:, cb * 4 + j:cb * 4 + j + 1].broadcast_to([128, 128]),
                ALU.mult), sem="ac_v")
    return t_c


def _sc(p, fn):
    t = p.op("scalar", fn, sem="sself")
    p.wait("scalar", "sself", t)
    return t


TB_BETA, TB_GCUM, TB_NEGB, TB_EG, TB_SK, TB_SU, TB_SQ, TB_ELB = \
    0, 64, 128, 192, 256, 320, 384, 448
TB_S, TB_T1, TB_G, TB_T3 = 512, 576, 640, 704


def emit_attn_precast(p, ab, ins, aps):
    for cb in range(16):
        par = cb % 2
        if cb >= 2:
            p.wait("gpsimd", f"pc_st{par}", p.cnt[f"pc_st{par}"])
        lw = p.dma("gpsimd", ab.wq[par][:].rearrange("p (a b) -> p a b", a=16),
                   ins["Wqkv_s"][cb].rearrange("(a p) i -> p a i", p=128),
                   sem=f"pc_ld{par}")
        p.wait("sync", f"pc_ld{par}", lw)
        p.dma("sync", aps["Wqkv_bf"][cb].rearrange("(a p) i -> p a i", p=128),
              ab.wq[par][:].rearrange("p (a b) -> p a b", a=16),
              sem=f"pc_st{par}")
    for hc in range(16):
        par = hc % 2
        p.wait("gpsimd", f"pc_st{par}", p.cnt[f"pc_st{par}"])
        lw = p.dma("gpsimd", ab.wz[par][:], ins["Wz_s"][hc], sem=f"pc_ld{par}")
        p.wait("sync", f"pc_ld{par}", lw)
        p.dma("sync", aps["Wz_bf"][hc], ab.wz[par][:], sem=f"pc_st{par}")
    return p.cnt["pc_st0"] + p.cnt["pc_st1"]


def emit_attn_tables(p, ab, sb, hf, xf_ld):
    tb = ab.tab
    p.sem("at_t")
    p.wait("tensor", "xf_ld", xf_ld)
    for sname in ("ap_s", "dl_v", "dl_s"):
        if p.cnt.get(sname, 0) > 0:
            p.wait("tensor", sname, p.cnt[sname])
    if p.cnt.get("dl_t", 0) > 0:
        p.wait("vector", "dl_t", p.cnt["dl_t"])
        p.wait("scalar", "dl_t", p.cnt["dl_t"])
        p.wait("vector", "dl_v", p.cnt["dl_v"])
    for tt in range(8):
        for hc in range(16):
            p.op("tensor", lambda e, tt=tt, hc=hc: e.matmul(
                ab.ps[0][:, tt * 16:tt * 16 + 16],
                ab.x_fm[:, hc * 1024 + tt * 128:hc * 1024 + tt * 128 + 128],
                ab.wba[:, hc * 16:hc * 16 + 16],
                start=(hc == 0), stop=(hc == 15)),
                sem="at_t" if (tt == 7 and hc == 15) else None)
    t_ba = p.cnt["at_t"]
    p.wait("scalar", "at_t", t_ba)
    bview = ab.ps[0][:, 0:128].rearrange("p (t c) -> p t c", c=16)[:, :, 0:8]
    aview = ab.ps[0][:, 0:128].rearrange("p (t c) -> p t c", c=16)[:, :, 8:16]
    _sc(p, lambda e: e.activation(
        tb[:, TB_BETA:TB_BETA + 64].rearrange("p (t c) -> p t c", c=8),
        bview, AF.Sigmoid))
    p.wait("vector", "sself", p.cnt["sself"])
    _vc(p, lambda e: e.tensor_scalar(
        tb[:, TB_NEGB:TB_NEGB + 64], tb[:, TB_BETA:TB_BETA + 64],
        -1.0, None, ALU.mult))
    _vc(p, lambda e: e.tensor_tensor(
        tb[:, TB_S:TB_S + 64].rearrange("p (t c) -> p t c", c=8), aview,
        ab.dtb[:].unsqueeze(1).broadcast_to([128, 8, 8]), ALU.add))
    p.wait("scalar", "vself", p.cnt["vself"])
    _sc(p, lambda e: e.activation(tb[:, TB_T1:TB_T1 + 64], tb[:, TB_S:TB_S + 64],
                                  AF.Abs))
    _sc(p, lambda e: e.activation(tb[:, TB_T3:TB_T3 + 64], tb[:, TB_T1:TB_T1 + 64],
                                  AF.Exp, scale=sb.kc[:, 4:5]))
    _sc(p, lambda e: e.activation(tb[:, TB_T1:TB_T1 + 64], tb[:, TB_T3:TB_T3 + 64],
                                  AF.Ln, bias=sb.kc[:, 5:6]))
    _sc(p, lambda e: e.activation(tb[:, TB_T3:TB_T3 + 64], tb[:, TB_S:TB_S + 64],
                                  AF.Relu))
    p.wait("vector", "sself", p.cnt["sself"])
    _vc(p, lambda e: e.tensor_tensor(tb[:, TB_S:TB_S + 64], tb[:, TB_T1:TB_T1 + 64],
                                     tb[:, TB_T3:TB_T3 + 64], ALU.add))
    _vc(p, lambda e: e.tensor_tensor(
        tb[:, TB_G:TB_G + 64].rearrange("p (t c) -> p t c", c=8),
        tb[:, TB_S:TB_S + 64].rearrange("p (t c) -> p t c", c=8),
        ab.negA[:].unsqueeze(1).broadcast_to([128, 8, 8]), ALU.mult))
    p.wait("tensor", "vself", p.cnt["vself"])
    for tt in range(8):
        p.op("tensor", lambda e, tt=tt: e.matmul(
            ab.ps[2][:, tt * 8:tt * 8 + 8], ab.u1bd[:],
            tb[:, TB_G + tt * 8:TB_G + tt * 8 + 8], start=True, stop=True),
            sem="at_t" if tt == 7 else None)
    t_gc = p.cnt["at_t"]
    p.wait("vector", "at_t", t_gc)
    _vc(p, lambda e: e.tensor_copy(tb[:, TB_GCUM:TB_GCUM + 64], ab.ps[2][:, 0:64]))
    p.wait("tensor", "vself", p.cnt["vself"])
    for tt in range(8):
        p.op("tensor", lambda e, tt=tt: e.matmul(
            ab.ps[4][:, tt * 8:tt * 8 + 8], ab.sel64[:],
            tb[:, TB_GCUM + tt * 8:TB_GCUM + tt * 8 + 8], start=True, stop=True),
            sem="at_t" if tt == 7 else None)
    t_gl = p.cnt["at_t"]
    p.wait("vector", "at_t", t_gl)
    _vc(p, lambda e: e.tensor_tensor(tb[:, TB_T3:TB_T3 + 64], ab.ps[4][:, 0:64],
                                     tb[:, TB_GCUM:TB_GCUM + 64], ALU.subtract))
    p.wait("scalar", "vself", p.cnt["vself"])
    _sc(p, lambda e: e.activation(tb[:, TB_EG:TB_EG + 64],
                                  tb[:, TB_GCUM:TB_GCUM + 64], AF.Exp))
    _sc(p, lambda e: e.activation(tb[:, TB_SK:TB_SK + 64],
                                  tb[:, TB_T3:TB_T3 + 64], AF.Exp))
    p.wait("vector", "sself", p.cnt["sself"])
    _vc(p, lambda e: e.tensor_tensor(tb[:, TB_SU:TB_SU + 64],
                                     tb[:, TB_BETA:TB_BETA + 64],
                                     tb[:, TB_EG:TB_EG + 64], ALU.mult))
    _vc(p, lambda e: e.tensor_scalar(tb[:, TB_SQ:TB_SQ + 64],
                                     tb[:, TB_EG:TB_EG + 64],
                                     float(1.0 / np.sqrt(128.0)), None, ALU.mult))
    _vc(p, lambda e: e.tensor_tensor(tb[:, TB_ELB:TB_ELB + 64],
                                     tb[:, TB_SK:TB_SK + 64],
                                     tb[:, TB_EG:TB_EG + 64], ALU.mult))
    return p.cnt["vself"]


PROJ_PARTS = "qcdtz"


def emit_attn_proj(p, ab, sb, aps, hf):
    sq = "ap_ldq"
    st_ = "ap_t"
    sv = "ap_v"
    ss = "ap_s"
    for x in (sq, st_, sv, ss):
        p.sem(x)
    first = p.cnt[st_] == 0
    p.wait("tensor", "sself", p.cnt["sself"])
    p.wait("tensor", "vself", p.cnt["vself"])
    if p.cnt.get("dl_t", 0) > 0:
        p.wait("scalar", "dl_t", p.cnt["dl_t"])
        p.wait("scalar", "dl_v", p.cnt["dl_v"])
        p.wait("tensor", "dl_v", p.cnt["dl_v"])
    if p.cnt.get("ep_st", 0) > 0:
        p.wait("scalar", "ep_st", p.cnt["ep_st"])
        p.wait("vector", "ep_st", p.cnt["ep_st"])
    if p.cnt[ss] > 0:
        p.wait("tensor", ss, p.cnt[ss])
        p.wait("scalar", sv, p.cnt[sv])
    for par_ in range(2):
        if p.cnt.get(f"pc_st{par_}", 0) > 0:
            p.wait("sync", f"pc_st{par_}", p.cnt[f"pc_st{par_}"])
    if hf > 0 and "t" in PROJ_PARTS:
        p.wait("vector", "dl_t", p.cnt["dl_t"])
        p.wait("vector", "dl_s", p.cnt["dl_s"])
        if p.cnt.get("ep_st", 0) > 0:
            p.wait("vector", "ep_st", p.cnt["ep_st"])
    for cb in range(16):
        par = cb % 2
        if p.cnt[ss] >= 8:
            p.wait("sync", ss, p.cnt[ss] - 6)
        lw = p.dma("sync", ab.wq[par][:].rearrange("p (a b) -> p a b", a=16),
                   aps["Wqkv_bf"][cb].rearrange("(a p) i -> p a i", p=128),
                   sem=f"{sq}{par}")
        p.wait("tensor", f"{sq}{par}", lw)
        for tg in range(2):
            pb_ = ab.ps[cb % 2]
            if tg == 1:
                p.wait("tensor", ss, p.cnt[ss])
            elif p.cnt[ss] >= 8:
                p.wait("tensor", ss, p.cnt[ss] - 6)
            for hc in range(16):
                p.op("tensor", lambda e, par=par, hc=hc, tg=tg, pb_=pb_: e.matmul(
                    pb_[:], ab.wq[par][:, hc * 128:hc * 128 + 128],
                    ab.x_fm[:, hc * 1024 + tg * 512:hc * 1024 + tg * 512 + 512],
                    start=(hc == 0), stop=(hc == 15)),
                    sem=st_ if hc == 15 else None)
            t_q = p.cnt[st_]
            p.wait("scalar", st_, t_q)
            p.op("scalar", lambda e, cb=cb, tg=tg, pb_=pb_: e.copy(
                ab.raw[:, cb * 1032 + 8 + tg * 512:cb * 1032 + 8 + tg * 512 + 512],
                pb_[:]), sem=ss)
        if "c" not in PROJ_PARTS:
            continue
        if hf > 0 and "t" in PROJ_PARTS:
            p.op("vector", lambda e, cb=cb: e.tensor_copy(
                ab.raw[:, cb * 1032 + 5:cb * 1032 + 8],
                ab.tailb[:, cb * 3:cb * 3 + 3]), sem=sv)
            p.wait("tensor", sv, p.cnt[sv])
        p.wait("tensor", ss, p.cnt[ss])
        for tg in range(2):
            pc = ab.ps[2 + cb % 2]
            if p.cnt[ss] > 0:
                p.wait("tensor", ss, p.cnt[ss])
            for j in range(4):
                p.op("tensor", lambda e, cb=cb, tg=tg, j=j, pc=pc: e.matmul(
                    pc[:], ab.cdiag[:, (cb * 4 + j) * 128:(cb * 4 + j) * 128 + 128],
                    ab.raw[:, cb * 1032 + 5 + tg * 512 + j:
                           cb * 1032 + 5 + tg * 512 + j + 512],
                    start=(j == 0), stop=(j == 3)),
                    sem=st_ if j == 3 else None)
            t_c = p.cnt[st_]
            if "d" not in PROJ_PARTS:
                continue
            p.wait("scalar", st_, t_c)
            if cb < 4:
                dst = ab.q_dm[:, cb * 1024 + tg * 512:cb * 1024 + tg * 512 + 512]
            elif cb < 8:
                dst = ab.k_dm[:, (cb - 4) * 1024 + tg * 512:
                              (cb - 4) * 1024 + tg * 512 + 512]
            else:
                dst = ab.v_dm[:, (cb - 8) * 1024 + tg * 512:
                              (cb - 8) * 1024 + tg * 512 + 512]
            _af = AF.Silu
            p.op("scalar", lambda e, dst=dst, pc=pc, _af=_af: e.activation(
                dst, pc[:], _af), sem=ss)
        if "t" in PROJ_PARTS:
            p.wait("vector", st_, p.cnt[st_])
            p.wait("vector", ss, p.cnt[ss])
            p.op("vector", lambda e, cb=cb: e.tensor_copy(
                ab.tailb[:, cb * 3:cb * 3 + 3],
                ab.raw[:, cb * 1032 + 1029:cb * 1032 + 1032]), sem=sv)
    if "z" not in PROJ_PARTS:
        return p.cnt[ss], p.cnt[sv]
    for zh in range(2):
        for hc in range(16):
            par = hc % 2
            if hc >= 2 or zh == 1:
                p.wait("sync", st_, p.cnt[st_] - 1)
            lw = p.dma("sync", ab.wz[par][:, 0:512],
                       aps["Wz_bf"][hc][:, zh * 512:zh * 512 + 512],
                       sem=f"ap_ldz{par}")
            p.wait("tensor", f"ap_ldz{par}", lw)
            if hc == 0:
                p.wait("tensor", ss, p.cnt[ss])
            for tt in range(8):
                p.op("tensor", lambda e, par=par, hc=hc, tt=tt: e.matmul(
                    ab.ps[tt][:], ab.x_fm[:, hc * 1024 + tt * 128:
                                          hc * 1024 + tt * 128 + 128],
                    ab.wz[par][:, 0:512],
                    start=(hc == 0), stop=(hc == 15)),
                    sem=st_ if tt == 7 else None)
        t_z = p.cnt[st_]
        p.wait("scalar", st_, t_z)
        for bb in range(4):
            zv = ab.z_act[:].rearrange("p (t c) -> p t c", c=1024)[
                :, 2 * bb:2 * bb + 2, zh * 512:zh * 512 + 512]
            pv = ab.ps1k[bb][:].rearrange("p (t c) -> p t c", c=512)
            p.op("scalar", lambda e, zv=zv, pv=pv: e.activation(
                zv, pv, AF.Silu), sem=ss)
    return p.cnt[ss], p.cnt[sv]


def _fv(p, fn):
    t = p.op("vector", fn, sem="dl_v")
    p.wait("vector", "dl_v", t)
    return t


def emit_attn_delta(p, ab, sb, hf):
    DT, DV_, DS = "dl_t", "dl_v", "dl_s"
    for x in (DT, DV_, DS):
        p.sem(x)
    tb = ab.tab
    if p.cnt.get("ap_s", 0) > 0:
        p.wait("tensor", "ap_s", p.cnt["ap_s"])
    if p.cnt.get("ap_v", 0) > 0:
        p.wait("tensor", "ap_v", p.cnt["ap_v"])

    def v3(dst2d):
        return dst2d.rearrange("p (a b) -> p a b", b=128)

    def v4(dst2d):
        return dst2d.rearrange("p (a r b) -> p a r b", r=2, b=128)

    def v4c(dst2d):
        return dst2d.rearrange("p (c a b) -> p c a b", c=2, b=128)

    def m4(mask):
        return mask.rearrange("p (a b) -> p a b", b=128) \
            .unsqueeze(1).broadcast_to([128, 2, 4, 128])

    iv3 = sb.ident[:].unsqueeze(1).broadcast_to([128, 4, 128])

    for pr in range(8):
        c2 = pr * 128
        tcol = pr * 8
        p.wait("tensor", DV_, p.cnt[DV_])
        p.wait("tensor", DS, p.cnt[DS])
        for kh in range(4):
            p.op("tensor", lambda e, kh=kh, c2=c2: e.matmul(
                ab.ps[6][:, kh * 128:kh * 128 + 128],
                ab.k_dm[:, kh * 1024 + c2:kh * 1024 + c2 + 128],
                sb.ident[:], start=True, stop=True),
                sem=DT if kh == 3 else None)
        mt_ktm = p.cnt[DT]
        for vh in range(8):
            p.op("tensor", lambda e, vh=vh, c2=c2: e.matmul(
                ab.ps1k[2][:, vh * 128:vh * 128 + 128],
                ab.v_dm[:, vh * 1024 + c2:vh * 1024 + c2 + 128],
                sb.ident[:], start=True, stop=True),
                sem=DT if vh == 7 else None)
        mt_vtm = p.cnt[DT]
        for kh in range(4):
            p.op("tensor", lambda e, kh=kh, c2=c2: e.matmul(
                ab.ps[0][:, kh * 128:kh * 128 + 128],
                ab.k_dm[:, kh * 1024 + c2:kh * 1024 + c2 + 128],
                ab.k_dm[:, kh * 1024 + c2:kh * 1024 + c2 + 128],
                start=True, stop=True), sem=DT if kh == 3 else None)
        mt_sym = p.cnt[DT]
        for kh in range(4):
            p.op("tensor", lambda e, kh=kh, c2=c2: e.matmul(
                ab.ps[1][:, kh * 128:kh * 128 + 128],
                ab.k_dm[:, kh * 1024 + c2:kh * 1024 + c2 + 128],
                ab.q_dm[:, kh * 1024 + c2:kh * 1024 + c2 + 128],
                start=True, stop=True), sem=DT if kh == 3 else None)
        mt_symq = p.cnt[DT]
        pkv = tb[:, TB_GCUM + tcol:TB_GCUM + tcol + 72] \
            .rearrange("p (a b) -> p a b", b=8)[:, 0:9:8, :]
        pkb = tb[:, TB_T3:TB_T3 + 16]
        t_c = _fv(p, lambda e, pkv=pkv, pkb=pkb: e.tensor_copy(
            pkb.rearrange("p (a b) -> p a b", b=8), pkv))
        p.wait("tensor", DV_, t_c)
        t_pk = p.op("tensor", lambda e, pkb=pkb: e.matmul(
            ab.ps[7][0:16, 0:128], pkb, ab.identf, start=True, stop=True),
            sem=DT)
        p.wait("vector", DT, t_pk)
        t_rows = _fv(p, lambda e: e.tensor_copy(
            ab.rows[:], ab.ps[7][0:16, 0:128]))
        p.wait("tensor", DV_, t_rows)
        for vh in range(8):
            p.op("tensor", lambda e, vh=vh: e.matmul(
                ab.ps1k[1][:, vh * 128:vh * 128 + 128],
                ab.sel[:, vh * 128:vh * 128 + 128], ab.rows[:],
                start=True, stop=True), sem=DT if vh == 7 else None)
        mt_R = p.cnt[DT]
        p.wait("vector", DT, mt_vtm)
        bv = tb[:, TB_BETA + tcol:TB_BETA + tcol + 8] \
            .unsqueeze(2).broadcast_to([128, 8, 128])
        t_vbc = _fv(p, lambda e, bv=bv: e.tensor_tensor(
            v3(ab.vbc[:]), v3(ab.ps1k[2][:]), bv, ALU.mult))
        p.wait("tensor", DV_, t_vbc)
        for vh in range(8):
            p.op("tensor", lambda e, vh=vh: e.matmul(
                ab.ps1k[2][:, vh * 128:vh * 128 + 128],
                ab.sel[:, 1024 + vh * 128:1024 + vh * 128 + 128], ab.rows[:],
                start=True, stop=True), sem=DT if vh == 7 else None)
        mt_Bj = p.cnt[DT]
        p.wait("vector", DT, mt_ktm)
        t_ktm = _fv(p, lambda e: e.tensor_copy(
            ab.kTM[:], ab.ps[6][:, 0:512]))
        p.wait("vector", DT, mt_R)
        giv = tb[:, TB_GCUM + tcol:TB_GCUM + tcol + 8] \
            .unsqueeze(2).broadcast_to([128, 8, 128])
        t = _fv(p, lambda e, giv=giv: e.tensor_tensor(
            v3(ab.E1m[:]), v3(ab.ps1k[1][:]), giv, ALU.subtract))
        p.wait("scalar", DV_, t)
        t = p.op("scalar", lambda e: e.activation(
            ab.rec[:], ab.E1m[:], AF.Abs), sem=DS)
        p.wait("scalar", DS, t)
        t = p.op("scalar", lambda e, sb=sb: e.activation(
            ab.E1m[:], ab.rec[:], AF.Exp, scale=sb.kc[:, 4:5]), sem=DS)
        p.wait("vector", DS, t)
        _fv(p, lambda e: e.tensor_tensor(
            v4c(ab.D2[:]), v4c(ab.E1m[:]), m4(ab.mupi4), ALU.mult))
        nbv = tb[:, TB_NEGB + tcol:TB_NEGB + tcol + 8] \
            .unsqueeze(2).broadcast_to([128, 8, 128])
        _fv(p, lambda e, nbv=nbv: e.tensor_tensor(
            v3(ab.Dst[:]), v3(ab.E1m[:]), nbv, ALU.mult))
        t = _fv(p, lambda e: e.tensor_tensor(
            v4c(ab.D1[:]), v4c(ab.Dst[:]), m4(ab.mlow4), ALU.mult))
        p.wait("vector", DT, mt_Bj)
        p.wait("vector", DV_, t)
        _fv(p, lambda e: e.tensor_tensor(
            v3(ab.Dst[:]), v3(ab.E1m[:]), v3(ab.ps1k[2][:]), ALU.mult))
        t = _fv(p, lambda e: e.tensor_tensor(
            v4c(ab.D1T[:]), v4c(ab.Dst[:]), m4(ab.mups4), ALU.mult))
        symv = ab.ps[0][:].rearrange("p (a b) -> p a b", b=128) \
            .unsqueeze(2).broadcast_to([128, 4, 2, 128])
        symqv = ab.ps[1][:].rearrange("p (a b) -> p a b", b=128) \
            .unsqueeze(2).broadcast_to([128, 4, 2, 128])
        p.wait("vector", DT, mt_symq)
        _fv(p, lambda e, symv=symv: e.tensor_tensor(
            v4(ab.Ppp[0][:]), symv, v4(ab.D1[:]), ALU.mult))
        _fv(p, lambda e, symv=symv: e.tensor_tensor(
            v4(ab.Qpp[0][:]), symv, v4(ab.D1T[:]), ALU.mult))
        t = _fv(p, lambda e, symqv=symqv: e.tensor_tensor(
            v4(ab.aT[:]), symqv, v4(ab.D2[:]), ALU.mult))
        p.wait("vector", DV_, t)
        t = _fv(p, lambda e: e.tensor_tensor(
            v4c(ab.A[0][:]), v4c(ab.Qpp[0][:]),
            iv3.unsqueeze(1).broadcast_to([128, 2, 4, 128]), ALU.add))
        for lvl in range(5):
            pa, pb2 = ab.Ppp[lvl % 2], ab.Ppp[(lvl + 1) % 2]
            qa, qb = ab.Qpp[lvl % 2], ab.Qpp[(lvl + 1) % 2]
            p.wait("tensor", DV_, p.cnt[DV_])
            for vh in range(8):
                p.op("tensor", lambda e, vh=vh, pa=pa, qa=qa: e.matmul(
                    ab.ps1k[0][:, vh * 128:vh * 128 + 128],
                    qa[:, vh * 128:vh * 128 + 128],
                    pa[:, vh * 128:vh * 128 + 128],
                    start=True, stop=True), sem=DT if vh == 7 else None)
            for vh in range(8):
                p.op("tensor", lambda e, vh=vh, pa=pa, qa=qa: e.matmul(
                    ab.ps1k[1][:, vh * 128:vh * 128 + 128],
                    pa[:, vh * 128:vh * 128 + 128],
                    qa[:, vh * 128:vh * 128 + 128],
                    start=True, stop=True), sem=DT if vh == 7 else None)
            mt_pq = p.cnt[DT]
            p.wait("vector", DT, mt_pq)
            _fv(p, lambda e, pb2=pb2: e.tensor_copy(
                pb2[:], ab.ps1k[0][:]))
            _fv(p, lambda e, qb=qb: e.tensor_copy(
                qb[:], ab.ps1k[1][:]))
            t = _fv(p, lambda e, lvl=lvl: e.tensor_tensor(
                v4c(ab.A[lvl + 1][:]), v4c(ab.ps1k[1][:]),
                iv3.unsqueeze(1).broadcast_to([128, 2, 4, 128]), ALU.add))
            p.wait("vector", DV_, t)
        for ci in range(2):
            ch = c2 + ci * 64
            cf = ci * 64
            sub = tb[cf:cf + 64, TB_SU + tcol:TB_SU + tcol + 8] \
                .unsqueeze(2).broadcast_to([64, 8, 128])
            sqb = tb[cf:cf + 64, TB_SQ + tcol:TB_SQ + tcol + 8] \
                .unsqueeze(2).broadcast_to([64, 8, 128])
            skb = tb[cf:cf + 64, TB_SK + tcol:TB_SK + tcol + 8] \
                .rearrange("p (a r) -> p a r", r=2) \
                .unsqueeze(3).broadcast_to([64, 4, 2, 128])
            p.wait("tensor", DV_, p.cnt[DV_])
            for vh in range(8):
                p.op("tensor", lambda e, vh=vh, ch=ch, cf=cf: e.matmul(
                    ab.ps1k[2][cf:cf + 64, vh * 128:vh * 128 + 128],
                    ab.k_dm[:, (vh // 2) * 1024 + ch:(vh // 2) * 1024 + ch + 64],
                    ab.state_bf[:, vh * 128:vh * 128 + 128],
                    start=True, stop=True), sem=DT if vh == 7 else None)
            mt_upre = p.cnt[DT]
            p.wait("vector", DT, mt_upre)
            t = _fv(p, lambda e, cf=cf, sub=sub: e.tensor_tensor(
                v3(ab.tsb[cf:cf + 64, 0:1024]), v3(ab.ps1k[2][cf:cf + 64, :]),
                sub, ALU.mult))
            p.wait("vector", DV_, t)
            t_u0 = _fv(p, lambda e, cf=cf: e.tensor_tensor(
                ab.u[0][cf:cf + 64, :], ab.vbc[cf:cf + 64, :],
                ab.tsb[cf:cf + 64, 0:1024], ALU.subtract))
            ucur = 0
            p.wait("tensor", DV_, t_u0)
            for lvl in range(6):
                for vh in range(8):
                    p.op("tensor", lambda e, vh=vh, cf=cf, lvl=lvl, ucur=ucur:
                         e.matmul(
                             ab.ps1k[3][cf:cf + 64, vh * 128:vh * 128 + 128],
                             ab.A[lvl][cf:cf + 64,
                                       vh * 128 + cf:vh * 128 + cf + 64],
                             ab.u[ucur][cf:cf + 64, vh * 128:vh * 128 + 128],
                             start=True, stop=True), sem=DT if vh == 7 else None)
                mt_app = p.cnt[DT]
                p.wait("scalar", DT, mt_app)
                dst = ab.vnew if lvl == 5 else ab.u[1 - ucur]
                t_ev = p.op("scalar", lambda e, cf=cf, dst=dst: e.copy(
                    dst[cf:cf + 64, :], ab.ps1k[3][cf:cf + 64, :]), sem=DS)
                p.wait("tensor", DS, t_ev)
                ucur = 1 - ucur
            for vh in range(8):
                p.op("tensor", lambda e, vh=vh, ch=ch, cf=cf: e.matmul(
                    ab.ps1k[2][cf:cf + 64, vh * 128:vh * 128 + 128],
                    ab.q_dm[:, (vh // 2) * 1024 + ch:(vh // 2) * 1024 + ch + 64],
                    ab.state_bf[:, vh * 128:vh * 128 + 128],
                    start=True, stop=True), sem=DT if vh == 7 else None)
            mt_o1 = p.cnt[DT]
            for vh in range(8):
                p.op("tensor", lambda e, vh=vh, cf=cf: e.matmul(
                    ab.ps1k[3][cf:cf + 64, vh * 128:vh * 128 + 128],
                    ab.aT[cf:cf + 64, vh * 128 + cf:vh * 128 + cf + 64],
                    ab.vnew[cf:cf + 64, vh * 128:vh * 128 + 128],
                    start=True, stop=True), sem=DT if vh == 7 else None)
            mt_o2 = p.cnt[DT]
            p.wait("vector", DT, mt_o1)
            t = _fv(p, lambda e, cf=cf, sqb=sqb: e.tensor_tensor(
                v3(ab.tsb[cf:cf + 64, 0:1024]), v3(ab.ps1k[2][cf:cf + 64, :]),
                sqb, ALU.mult))
            p.wait("vector", DT, mt_o2)
            p.wait("vector", DV_, t)
            t_out = _fv(p, lambda e, cf=cf, pr=pr: e.tensor_tensor(
                ab.outb[cf:cf + 64, pr * 1024:pr * 1024 + 1024],
                ab.tsb[cf:cf + 64, 0:1024], ab.ps1k[3][cf:cf + 64, :], ALU.add))
            krep = ab.kTM[cf:cf + 64, :].rearrange("p (a b) -> p a b", b=128) \
                .unsqueeze(2).broadcast_to([64, 4, 2, 128])
            t = _fv(p, lambda e, cf=cf, krep=krep, skb=skb: e.tensor_tensor(
                v4(ab.kmod[cf:cf + 64, :]), krep, skb, ALU.mult))
            p.wait("tensor", DV_, t)
            for vh in range(8):
                p.op("tensor", lambda e, vh=vh, cf=cf: e.matmul(
                    ab.ps1k[2][:, vh * 128:vh * 128 + 128],
                    ab.kmod[cf:cf + 64, vh * 128:vh * 128 + 128],
                    ab.vnew[cf:cf + 64, vh * 128:vh * 128 + 128],
                    start=True, stop=True), sem=DT if vh == 7 else None)
            mt_su = p.cnt[DT]
            t_eb = p.op("tensor", lambda e, cf=cf, tcol=tcol: e.matmul(
                ab.ps[7][:, 0:8], sb.onesb[cf:cf + 1, 0:128],
                tb[cf:cf + 1, TB_ELB + tcol:TB_ELB + tcol + 8],
                start=True, stop=True), sem=DT)
            p.wait("vector", DT, t_eb)
            t = _fv(p, lambda e: e.tensor_copy(ab.elb[:], ab.ps[7][:, 0:8]))
            p.wait("vector", DV_, t)
            ebv = ab.elb[:].unsqueeze(2).broadcast_to([128, 8, 128])
            t = _fv(p, lambda e, ebv=ebv: e.tensor_tensor(
                v3(ab.tsb[:]), v3(ab.state[:]), ebv, ALU.mult))
            p.wait("vector", DT, mt_su)
            p.wait("vector", DV_, t)
            t = _fv(p, lambda e: e.tensor_tensor(
                ab.state[:], ab.tsb[:], ab.ps1k[2][:], ALU.add))
            p.wait("vector", DV_, t)
            _fv(p, lambda e: e.tensor_copy(ab.state_bf[:], ab.state[:]))
    return p.cnt[DV_]


def emit_attn_epilogue(p, ab, sb, aps, hf):
    DV_, DS = "dl_v", "dl_s"
    p.sem("ep_st")
    if "ap_s" in p.cnt:
        p.wait("vector", "ap_s", p.cnt["ap_s"])
    nwv = ab.nwbc[:].unsqueeze(1).broadcast_to([128, 8, 128])

    def r3(x):
        return x.rearrange("p (a b) -> p a b", b=128)

    for pr in range(8):
        t = _fv(p, lambda e, pr=pr: e.tensor_tensor(
            ab.tsb[:], ab.outb[:, pr * 1024:pr * 1024 + 1024],
            ab.z_act[:, pr * 1024:pr * 1024 + 1024], ALU.mult))
        p.wait("scalar", DV_, t)
        t = p.op("scalar", lambda e: e.activation(
            ab.rec[:, 0:1024], ab.tsb[:], AF.Square), sem=DS)
        p.wait("vector", DS, t)
        _fv(p, lambda e, pr=pr: e.reduce_sum(
            ab.epiv[:, pr * 8:pr * 8 + 8], r3(ab.rec[:, 0:1024]),
            axis=AX.X))
    sbb = sb.sbb
    p.wait("vector", DV_, p.cnt[DV_])
    _vc(p, lambda e: e.tensor_scalar(sbb[:, 0:64], ab.epiv[:], 1.0 / 128.0,
                                     None, ALU.mult))
    _vc(p, lambda e: e.tensor_scalar(sbb[:, 0:64], sbb[:, 0:64], float(RMS_EPS),
                                     None, ALU.add))
    I32 = mybir.dt.int32
    _vc(p, lambda e: e.tensor_scalar(
        sbb[:, 64:128].bitcast(I32), sbb[:, 0:64].bitcast(I32), 1, None,
        ALU.logical_shift_right))
    _vc(p, lambda e: e.tensor_scalar(
        sbb[:, 64:128].bitcast(I32), sbb[:, 64:128].bitcast(I32), 0, None,
        ALU.bitwise_not))
    _vc(p, lambda e: e.tensor_scalar(
        ab.epiv[:].bitcast(I32), sbb[:, 64:128].bitcast(I32),
        0x5F3759DF + 1, None, ALU.add))
    for _ in range(3):
        _vc(p, lambda e: e.tensor_tensor(sbb[:, 64:128], ab.epiv[:], ab.epiv[:],
                                         ALU.mult))
        _vc(p, lambda e: e.tensor_tensor(sbb[:, 64:128], sbb[:, 64:128],
                                         sbb[:, 0:64], ALU.mult))
        _vc(p, lambda e: e.tensor_scalar(sbb[:, 64:128], sbb[:, 64:128],
                                         -0.5, 1.5, ALU.mult, ALU.add))
        _vc(p, lambda e: e.tensor_tensor(ab.epiv[:], ab.epiv[:], sbb[:, 64:128],
                                         ALU.mult))
    for pr in range(8):
        t = _fv(p, lambda e, pr=pr: e.tensor_tensor(
            ab.tsb[:], ab.outb[:, pr * 1024:pr * 1024 + 1024],
            ab.z_act[:, pr * 1024:pr * 1024 + 1024], ALU.mult))
        rsv = ab.epiv[:, pr * 8:pr * 8 + 8].unsqueeze(2)             .broadcast_to([128, 8, 128])
        p.wait("vector", DV_, t)
        t = _fv(p, lambda e, rsv=rsv: e.tensor_tensor(
            r3(ab.rec[:, 0:1024]), r3(ab.tsb[:]), rsv, ALU.mult))
        if p.cnt["ep_st"] >= 16:
            p.wait("vector", "ep_st", p.cnt["ep_st"])
        p.wait("vector", DV_, t)
        t = _fv(p, lambda e, nwv=nwv: e.tensor_tensor(
            r3(ab.u[0][:]), r3(ab.rec[:, 0:1024]), nwv, ALU.mult))
        p.wait("sync", DV_, t)
        p.dma("sync",
              aps["i2b"][hf * 1024 + pr * 128:hf * 1024 + pr * 128 + 128, :],
              ab.u[0][:], sem="ep_st")
    return p.cnt["ep_st"]


def emit_attn(p, ab, sb, aps, nhalves=8, stages="APDE"):
    p.sem("xf_ld")
    p.sem("ap_s")
    p.sem("ep_st")
    if p.cnt.get("ac_v", 0) > 0:
        p.wait("vector", "ac_v", p.cnt["ac_v"])
    t0 = p.op("vector", lambda e: e.memset(ab.state[:], 0.0), sem="dl_v")
    t0 = p.op("vector", lambda e: e.memset(ab.state_bf[:], 0.0), sem="dl_v")
    t0 = p.op("vector", lambda e: e.memset(ab.raw[:], 0.0), sem="ap_v")
    emit_attn_precast(p, ab, aps["ins"], aps)
    for hf in range(nhalves):
        if p.cnt.get("ap_t", 0) > 0:
            p.wait("sync", "ap_t", p.cnt["ap_t"])
        for sn in ("dl_t", "dl_v", "dl_s"):
            if p.cnt.get(sn, 0) > 0:
                p.wait("sync", sn, p.cnt[sn])
        if p.cnt.get("ep_st", 0) > 0:
            p.wait("sync", "ep_st", p.cnt["ep_st"])
        xf = None
        for hs in range(16):
            xf = p.dma("sync", ab.x_fm[:, hs * 1024:hs * 1024 + 1024],
                       aps["x_gath"][hf * 1024:hf * 1024 + 1024,
                                     hs * 128:hs * 128 + 128],
                       sem="xf_ld", transpose=True)
        if "A" in stages:
            emit_attn_tables(p, ab, sb, hf, xf)
        if "P" in stages:
            emit_attn_proj(p, ab, sb, aps, hf)
        if "D" in stages:
            emit_attn_delta(p, ab, sb, hf)
        if "E" in stages:
            emit_attn_epilogue(p, ab, sb, aps, hf)
    return p.cnt["ep_st"]



class WoutBufs:

    def __init__(self, p, ar, pb):
        ar.reset()
        self.wo = ar.bf16(32 * 2048)
        self.afm = ar.bf16(32 * 512)
        self.xs = [ar.f32(2048) for _ in range(2)]
        self.ev = [ar.f32(2048) for _ in range(2)]
        self.ps1k = pb.ps1k


def emit_wout(p, wb, sb, aps, slice_off):
    for s in ("wo_t", "wo_v"):
        p.sem(s)
    for eng in ("sync", "gpsimd", "tensor", "vector", "scalar"):
        for sname in ("dl_t", "dl_v", "dl_s", "ap_s", "ap_v", "ep_st"):
            if p.cnt.get(sname, 0) > 0:
                p.wait(eng, sname, p.cnt[sname])
    if p.cnt.get("cc", 0) > 0:
        p.wait("sync", "cc", p.cnt["cc"])
        p.wait("gpsimd", "cc", p.cnt["cc"])
    for ch in range(8):
        par = ch % 2
        p.dma("gpsimd",
              wb.wo[:, ch * 8192:ch * 8192 + 8192]
              .rearrange("p (a b) -> p a b", a=4),
              aps["W_out"].rearrange("(a p) i -> p a i", p=128)
              [:, ch * 4:ch * 4 + 4, :],
              sem=f"wo_ld{par}")
    for par in range(2):
        p.wait("tensor", f"wo_ld{par}", p.cnt[f"wo_ld{par}"])
    for q in range(4):
        if q > 0:
            p.wait("sync", "wo_t", p.cnt["wo_t"])
        af = None
        for fc in range(32):
            g, c0 = fc // 8, (fc % 8) * 128
            r0 = g * 8192 + slice_off + q * 512
            af = p.dma("sync", wb.afm[:, fc * 512:fc * 512 + 512],
                       aps["o2b"][r0:r0 + 512, c0:c0 + 128],
                       sem="af_ld", transpose=True)
        p.wait("tensor", "af_ld", af)
        for t in range(4):
            j = t % 2
            tok0 = q * 512 + t * 128
            if p.cnt["wo_v"] >= 2:
                p.wait("sync", "wo_v", p.cnt["wo_v"] - 2)
            xl = p.dma("sync", wb.xs[j][:], aps["x1"][tok0:tok0 + 128, :],
                       sem=f"wo_xs{j}")
            if p.cnt["wo_v"] >= 2:
                p.wait("tensor", "wo_v", p.cnt["wo_v"] - 2)
            for fc in range(32):
                for dq in range(4):
                    p.op("tensor", lambda e, fc=fc, dq=dq, j=j, t=t: e.matmul(
                        wb.ps1k[2 * j + dq // 2][:, (dq % 2) * 512:
                                                 (dq % 2) * 512 + 512],
                        wb.afm[:, fc * 512 + t * 128:fc * 512 + t * 128 + 128],
                        wb.wo[:, fc * 2048 + dq * 512:fc * 2048 + dq * 512 + 512],
                        start=(fc == 0), stop=(fc == 31)),
                        sem="wo_t" if (fc == 31 and dq == 3) else None)
            mt = p.cnt["wo_t"]
            p.wait("vector", "wo_t", mt)
            p.wait("vector", f"wo_xs{j}", xl)
            if p.cnt.get(f"wo_st{j}", 0) > 0:
                p.wait("vector", f"wo_st{j}", p.cnt[f"wo_st{j}"])
            for hh in range(2):
                p.op("vector", lambda e, j=j, hh=hh: e.tensor_tensor(
                    wb.ev[j][:, hh * 1024:hh * 1024 + 1024],
                    wb.ps1k[2 * j + hh][:],
                    wb.xs[j][:, hh * 1024:hh * 1024 + 1024], ALU.add),
                    sem="wo_v")
            p.wait("sync", "wo_v", p.cnt["wo_v"])
            p.dma("sync", aps["x2"][tok0:tok0 + 128, :], wb.ev[j][:],
                  sem=f"wo_st{j}")
    return p.cnt.get("wo_st0", 0), p.cnt.get("wo_st1", 0)


def host_wout(core, W_out):
    kh0 = (core % 4) * 4
    vhs = [2 * k + r for k in range(kh0, kh0 + 4) for r in range(2)]
    rows = [vh * 128 + d for vh in vhs for d in range(128)]
    return dict(Wo_c=np.ascontiguousarray(W_out[rows, :]))


class WoutPBufs:

    def __init__(self, p, ar, pb):
        ar.reset()
        self.wo = ar.bf16(8 * 2048)
        self.afm = [ar.bf16(8 * 512) for _ in range(2)]
        self.ev = [ar.f32(2048) for _ in range(2)]
        self.xs = [ar.f32(2048) for _ in range(2)]
        self.ps1k = pb.ps1k


def emit_wout_partial(p, wb, sb, aps):
    for s in ("wo_t", "wo_s"):
        p.sem(s)
    for eng in ("sync", "gpsimd", "tensor", "vector", "scalar"):
        for sname in ("dl_t", "dl_v", "dl_s", "ap_s", "ap_v", "ep_st"):
            if p.cnt.get(sname, 0) > 0:
                p.wait(eng, sname, p.cnt[sname])
    lw = p.dma("gpsimd", wb.wo[:].rearrange("p (a b) -> p a b", a=8),
               aps["Wo_c"].rearrange("(a p) i -> p a i", p=128), sem="wo_ld")
    p.wait("tensor", "wo_ld", lw)
    for tg in range(16):
        par = tg % 2
        if tg >= 2:
            p.wait("sync", "wo_t", p.cnt["wo_t"] - 4)
        af = None
        for fc in range(8):
            r0 = tg * 512
            af = p.dma("sync", wb.afm[par][:, fc * 512:fc * 512 + 512],
                       aps["i2b"][r0:r0 + 512, fc * 128:fc * 128 + 128],
                       sem=f"af_ld{par}", transpose=True)
        p.wait("tensor", f"af_ld{par}", af)
        for t in range(4):
            gt = tg * 4 + t
            j = gt % 2
            if p.cnt["wo_s"] >= 4:
                p.wait("tensor", "wo_s", p.cnt["wo_s"] - 2)
            for fc in range(8):
                for dq in range(4):
                    p.op("tensor", lambda e, fc=fc, dq=dq, j=j, t=t, par=par:
                         e.matmul(
                             wb.ps1k[2 * j + dq // 2][:, (dq % 2) * 512:
                                                      (dq % 2) * 512 + 512],
                             wb.afm[par][:, fc * 512 + t * 128:
                                         fc * 512 + t * 128 + 128],
                             wb.wo[:, fc * 2048 + dq * 512:
                                   fc * 2048 + dq * 512 + 512],
                             start=(fc == 0), stop=(fc == 7)),
                         sem="wo_t" if (fc == 7 and dq == 3) else None)
            mt = p.cnt["wo_t"]
            p.wait("scalar", "wo_t", mt)
            if p.cnt.get(f"wo_st{j}", 0) > 0:
                p.wait("scalar", f"wo_st{j}", p.cnt[f"wo_st{j}"])
            for hh in range(2):
                p.op("scalar", lambda e, j=j, hh=hh: e.copy(
                    wb.ev[j][:, hh * 1024:hh * 1024 + 1024],
                    wb.ps1k[2 * j + hh][:]), sem="wo_s")
            p.wait("sync", "wo_s", p.cnt["wo_s"])
            p.dma("sync", aps["pt"][gt * 128:gt * 128 + 128, :], wb.ev[j][:],
                  sem=f"wo_st{j}")
    return p.cnt.get("wo_st0", 0), p.cnt.get("wo_st1", 0)


def emit_x2add(p, wb, aps):
    p.wait("sync", "cc", p.cnt["cc"])
    for t in range(16):
        par = t % 2
        if p.cnt.get(f"x2_st{par}", 0) > 0:
            p.wait("sync", f"x2_st{par}", p.cnt[f"x2_st{par}"])
        l1 = p.dma("sync", wb.xs[par][:], aps["x1"][t * 128:t * 128 + 128, :],
                   sem=f"x2_l{par}")
        l2 = p.dma("sync", wb.ev[par][:], aps["rsb"][t * 128:t * 128 + 128, :],
                   sem=f"x2_r{par}")
        p.wait("vector", f"x2_l{par}", l1)
        p.wait("vector", f"x2_r{par}", l2)
        tv = p.op("vector", lambda e, par=par: e.tensor_tensor(
            wb.ev[par][:], wb.ev[par][:], wb.xs[par][:], ALU.add), sem="x2_v")
        p.wait("sync", "x2_v", tv)
        p.dma("sync", aps["x2"][t * 128:t * 128 + 128, :], wb.ev[par][:],
              sem=f"x2_st{par}")
    return p.cnt.get("x2_st0", 0), p.cnt.get("x2_st1", 0)


def emit_era_gate(p, sems=None):
    names = [s for s in (sems or list(p.cnt)) if p.cnt.get(s, 0) > 0]
    for eng in ENGINES:
        for s in names:
            p.wait(eng, s, p.cnt[s])



from contextlib import ExitStack
from concourse.bass_utils import run_bass_kernel_spmd


def _build():
    nc = bass.Bass("TRN2", target_bir_lowering=False)
    ins = {}

    def di(n, shp, dt=F32):
        ins[n] = nc.dram_tensor(n, shp, dt, kind="ExternalInput").ap()

    di("x", [TOK, D])
    di("Wg_s", [NIC, D, 128]); di("Wu_s", [NIC, D, 128])
    di("Wd_s", [4, NIC, 128, 512])
    di("gs2", [128, NIC]); di("us2", [128, NIC]); di("ds_row", [1, D])
    di("Wqkv_s", [16, D, 128]); di("Wz_s", [16, 128, 1024])
    di("Wba_c", [D, 16]); di("convw", [16, 128, 4])
    di("A_c", [1, 8]); di("dt_c", [1, 8]); di("norm_weight", [128])
    di("Wo_c", [1024, D])
    di("SEL", [16, 16, 128]); di("SEL64", [128, 128]); di("U1BD", [128, 128])
    di("MLOW4", [128, 512], BF16); di("MUPS4", [128, 512], BF16)
    di("MUPI4", [128, 512], BF16)
    di("IDENT", [128, 128], BF16); di("IDENTF", [128, 128]); di("ONESF", [1, 128])
    out = nc.dram_tensor("out", [TOK, D], F32, kind="ExternalOutput").ap()
    x1_t = nc.dram_tensor("x1", [TOK, D], F32)
    x2_t = nc.dram_tensor("x2", [TOK, D], F32)
    i1b_t = nc.dram_tensor("i1b", [TOK, D], BF16)
    xg_t = nc.dram_tensor("x_gath", [8192, D], BF16)
    i2b_t = nc.dram_tensor("i2b", [8192, 1024], BF16)
    pt_t = nc.dram_tensor("pt", [8192, D], F32)
    rsb_t = nc.dram_tensor("rsb", [TOK, D], F32)
    wqbf = nc.dram_tensor("Wqkv_bf", [16, D, 128], BF16)
    wzbf = nc.dram_tensor("Wz_bf", [16, 128, 1024], BF16)
    RG = [[0, 1, 2, 3], [4, 5, 6, 7]]
    with ExitStack() as ctx:
        p = Prog(nc, ctx)
        sb = SharedBufs(p)
        ar = Arena(p)
        pb = PsumBufs(p)
        fb = FFNBufs(p, ar, pb)
        sb.load_consts(p, ins)
        emit_ffn_consts(p, fb, sb, ins)
        emit_ffn_pass(p, fb, sb, ins["x"], x1_t.ap(), ins, "f1")
        p.wait("sync", "ffn_stx0", p.cnt["ffn_stx0"])
        p.wait("sync", "ffn_stx1", p.cnt["ffn_stx1"])
        emit_ln_store_pass(p, sb, fb, x1_t.ap(), i1b_t.ap())
        p.wait("gpsimd", "lnp_st0", p.cnt["lnp_st0"])
        p.wait("gpsimd", "lnp_st1", p.cnt["lnp_st1"])
        p.op("gpsimd", lambda e: e.collective_compute(
            "AllGather", mybir.AluOpType.bypass, replica_groups=RG,
            ins=[i1b_t.ap().opt()], outs=[xg_t.ap().opt()]), sem="cc")
        emit_era_gate(p)
        ab = AttnBufs(p, ar, pb)
        emit_attn_consts(p, ab, sb, ins)
        aps_a = dict(x_gath=xg_t.ap(), i2b=i2b_t.ap(), Wqkv_bf=wqbf.ap(),
                     Wz_bf=wzbf.ap(), ins=ins)
        p.wait("sync", "cc", 1)
        emit_attn(p, ab, sb, aps_a, nhalves=8, stages="APDE")
        wb = WoutPBufs(p, ar, pb)
        aps_w = dict(Wo_c=ins["Wo_c"], i2b=i2b_t.ap(), pt=pt_t.ap(),
                     x1=x1_t.ap(), rsb=rsb_t.ap(), x2=x2_t.ap())
        s0, s1 = emit_wout_partial(p, wb, sb, aps_w)
        p.wait("gpsimd", "wo_st0", s0)
        p.wait("gpsimd", "wo_st1", s1)
        p.op("gpsimd", lambda e: e.collective_compute(
            "ReduceScatter", mybir.AluOpType.add, replica_groups=RG,
            ins=[pt_t.ap().opt()], outs=[rsb_t.ap().opt()]), sem="cc")
        emit_x2add(p, wb, aps_w)
        emit_era_gate(p)
        emit_ffn_consts(p, fb, sb, ins)
        p.wait("tensor", "v_dsb", p.cnt["v_dsb"])
        emit_ffn_pass(p, fb, sb, x2_t.ap(), out, ins, "f2")
        p.wait("sync", "ffn_stx0", p.cnt["ffn_stx0"])
        p.wait("sync", "ffn_stx1", p.cnt["ffn_stx1"])
        p.run_block()
    return nc


_BUILT = None


def kernel(**inputs):
    global _BUILT
    if _BUILT is None:
        _BUILT = _build()
    nc = _BUILT
    hs = np.asarray(inputs["L0_attn_hidden_states"], dtype=np.float32)
    W = host_ffn_weights(
        np.asarray(inputs["L0_ffn_gate_w"], np.float32),
        np.asarray(inputs["L0_ffn_up_w"], np.float32),
        np.asarray(inputs["L0_ffn_down_w"], np.float32),
        np.asarray(inputs["L0_ffn_gate_s"], np.float32),
        np.asarray(inputs["L0_ffn_up_s"], np.float32),
        np.asarray(inputs["L0_ffn_down_s"], np.float32))
    C = host_consts()
    AC = host_attn_consts()
    nw = np.ascontiguousarray(np.asarray(inputs["L0_attn_norm_weight"],
                                         np.float32))
    in_maps = []
    for c in range(8):
        b, s = c // 4, c % 4
        A = host_attn_weights(
            c, np.asarray(inputs["L0_attn_W_qkvz"], np.float32),
            np.asarray(inputs["L0_attn_W_ba"], np.float32),
            np.asarray(inputs["L0_attn_conv_kernel"], np.float32),
            np.asarray(inputs["L0_attn_A_log"], np.float32),
            np.asarray(inputs["L0_attn_dt_bias"], np.float32))
        WO = host_wout(c, np.asarray(inputs["L0_attn_W_out"], np.float32))
        in_maps.append(dict(
            x=np.ascontiguousarray(hs[b, s * TOK:(s + 1) * TOK, :]),
            norm_weight=nw, **W, **A, **WO, **C, **AC))
    res = run_bass_kernel_spmd(nc, in_maps, core_ids=list(range(8)))
    full = np.empty((2, 8192, D), np.float32)
    for c in range(8):
        b, s = c // 4, c % 4
        full[b, s * TOK:(s + 1) * TOK, :] = res.results[c]["out"]
    return full


# version 03
# speedup vs baseline: 1.1235x; 1.1235x over previous
import numpy as np
import ml_dtypes
import concourse.bass as bass
import concourse.mybir as mybir

F32 = mybir.dt.float32
BF16 = mybir.dt.bfloat16
AF = mybir.ActivationFunctionType
ALU = mybir.AluOpType
AX = mybir.AxisListType
bf = ml_dtypes.bfloat16

ENGINES = ("sync", "tensor", "vector", "scalar", "gpsimd")

D = 2048
INTER = 5632
NIC = INTER // 128
TOK = 2048
HALF = 1024
LN_EPS = 1e-5
RMS_EPS = 1e-6


class Prog:

    def __init__(self, nc, ctx):
        self.nc = nc
        self.ctx = ctx
        self.ops = {e: [] for e in ENGINES}
        self.sems = {}
        self.cnt = {}
        self.waits = []

    def sem(self, name):
        if name not in self.sems:
            self.sems[name] = self.ctx.enter_context(self.nc.semaphore(name))
            self.cnt[name] = 0
        return self.sems[name]

    def dma(self, eng, out, in_, sem=None, transpose=False):
        tgt = None
        h = None
        if sem is not None:
            h = self.sem(sem)
            self.cnt[sem] += 16
            tgt = self.cnt[sem]

        def _f(e, out=out, in_=in_, h=h, t=transpose):
            i = e.dma_start(out=out, in_=in_, transpose=True) if t else \
                e.dma_start(out=out, in_=in_)
            if h is not None:
                i.then_inc(h, 16)
        self.ops[eng].append(_f)
        return tgt

    def wait(self, eng, sem, val):
        if val is None or val <= 0:
            return
        h = self.sem(sem)
        self.waits.append((eng, sem, val, len(self.ops[eng])))
        self.ops[eng].append(lambda e, h=h, v=val: e.wait_ge(h, v))

    def op(self, eng, fn, sem=None, by=1):
        tgt = None
        h = None
        if sem is not None:
            h = self.sem(sem)
            self.cnt[sem] += by
            tgt = self.cnt[sem]

        def _f(e, fn=fn, h=h, by=by):
            i = fn(e)
            if h is not None:
                i.then_inc(h, by)
        self.ops[eng].append(_f)
        return tgt

    def check(self):
        bad = [(e, s_, v, i) for (e, s_, v, i) in self.waits if v > self.cnt[s_]]
        for b in bad[:20]:
            print("OVERWAIT:", b, "final=", self.cnt[b[1]])
        assert not bad, f"{len(bad)} over-waits"

    def run_block(self):
        self.check()
        with self.nc.Block() as block:
            for e in ENGINES:
                fns = self.ops[e]
                if not fns:
                    continue

                def body(eng, fns=fns):
                    for f in fns:
                        f(eng)
                getattr(block, e)(body)



def host_ffn_weights(gate_w, up_w, down_w, gate_s, up_s, down_s):
    WgT = np.ascontiguousarray(gate_w.T)
    WuT = np.ascontiguousarray(up_w.T)
    Wg_s = np.ascontiguousarray(WgT.reshape(D, NIC, 128).transpose(1, 0, 2))
    Wu_s = np.ascontiguousarray(WuT.reshape(D, NIC, 128).transpose(1, 0, 2))
    WdT = down_w.T
    Wd_s = np.ascontiguousarray(WdT.reshape(NIC, 128, 4, 512).transpose(2, 0, 1, 3))
    gs2 = np.ascontiguousarray(gate_s.reshape(NIC, 128).T)
    us2 = np.ascontiguousarray(up_s.reshape(NIC, 128).T)
    return dict(Wg_s=Wg_s, Wu_s=Wu_s, Wd_s=Wd_s, gs2=gs2, us2=us2,
                ds_row=np.ascontiguousarray(down_s.reshape(1, D)))


def host_consts():
    return dict(
        IDENT=np.eye(128, dtype=bf),
        IDENTF=np.eye(128, dtype=np.float32),
        ONESF=np.ones((1, 128), dtype=np.float32),
    )



class SharedBufs:

    def __init__(self, p):
        nc, e = p.nc, p.ctx.enter_context
        self.ident = e(nc.sbuf_tensor("ident", [128, 128], BF16))
        self.onesf = e(nc.sbuf_tensor("onesf", [1, 128], F32))
        self.kc = e(nc.sbuf_tensor("kconsts", [128, 8], F32))
        self.onesb = e(nc.sbuf_tensor("onesb", [128, 128], F32))
        self.flush = e(nc.sbuf_tensor("flush", [128, 1], F32))
        self.sbb = e(nc.sbuf_tensor("sbb", [128, 128], F32))

    def load_consts(self, p, ins):
        p.dma("sync", self.ident[:], ins["IDENT"], sem="c_ld")
        t = p.dma("sync", self.onesf[:], ins["ONESF"], sem="c_ld")
        p.op("vector", lambda e, o=self.kc: e.memset(o[:, 0:1], float(LN_EPS)))
        p.op("vector", lambda e, o=self.kc: e.memset(o[:, 1:2], -0.5))
        p.op("vector", lambda e, o=self.kc: e.memset(o[:, 2:3], float(RMS_EPS)))
        p.op("vector", lambda e, o=self.kc: e.memset(
            o[:, 3:4].bitcast(mybir.dt.int32), 0x5F3759DF))
        p.op("vector", lambda e, o=self.kc: e.memset(o[:, 4:5], -1.0))
        p.op("vector", lambda e, o=self.onesb: e.memset(o[:], 1.0))
        p.op("vector", lambda e, o=self.kc: e.memset(o[:, 5:6], 1.0), sem="c_kc")
        return t


def emit_ln_stats_tile(p, sb, lb, par, ld_val, tt):
    xt, xo, sbb = lb.xt[par], lb.xln_tm[par], sb.sbb
    p.wait("vector", f"lnx{par}", ld_val)
    p.op("vector", lambda e, a=xt, o=sbb, tt=tt: e.reduce_sum(
        o[:, 2 * tt:2 * tt + 1], a[:], axis=AX.X))
    p.op("vector", lambda e, a=xt, o=xo: e.tensor_tensor(o[:], a[:], a[:], ALU.mult))
    t = p.op("vector", lambda e, a=xo, o=sbb, tt=tt: e.reduce_sum(
        o[:, 2 * tt + 1:2 * tt + 2], a[:], axis=AX.X), sem="ln_st")
    return t


def _vc(p, fn):
    t = p.op("vector", fn, sem="vself")
    p.wait("vector", "vself", t)


def emit_rstd_batch(p, sb, nt):
    sbb = sb.sbb
    ev = sbb[:, 0:2 * nt:2]
    od = sbb[:, 1:2 * nt:2]
    mu = sbb[:, 32:32 + nt]
    rs = sbb[:, 48:48 + nt]
    s1 = sbb[:, 64:64 + nt]
    s2 = sbb[:, 80:80 + nt]
    if p.cnt.get("c_kc", 0) and p.cnt.get("vself", 0) == 0:
        p.wait("vector", "c_kc", p.cnt["c_kc"])
    if p.cnt.get("ln_st", 0) > 0:
        p.wait("vector", "ln_st", p.cnt["ln_st"])
    _vc(p, lambda e: e.tensor_scalar(mu, ev, 1.0 / D, None, ALU.mult))
    _vc(p, lambda e: e.tensor_scalar(s1, od, 1.0 / D, None, ALU.mult))
    _vc(p, lambda e: e.tensor_tensor(s2, mu, mu, ALU.mult))
    _vc(p, lambda e: e.tensor_tensor(s1, s1, s2, ALU.subtract))
    _vc(p, lambda e: e.tensor_scalar(s1, s1, float(LN_EPS), None, ALU.add))
    I32 = mybir.dt.int32
    _vc(p, lambda e: e.tensor_scalar(
        s2.bitcast(I32), s1.bitcast(I32), 1, None, ALU.logical_shift_right))
    _vc(p, lambda e: e.tensor_scalar(
        s2.bitcast(I32), s2.bitcast(I32), 0, None, ALU.bitwise_not))
    _vc(p, lambda e: e.tensor_scalar(
        rs.bitcast(I32), s2.bitcast(I32), 0x5F3759DF + 1, None, ALU.add))
    for _ in range(3):
        _vc(p, lambda e: e.tensor_tensor(s2, rs, rs, ALU.mult))
        _vc(p, lambda e: e.tensor_tensor(s2, s2, s1, ALU.mult))
        _vc(p, lambda e: e.tensor_scalar(s2, s2, -0.5, 1.5, ALU.mult, ALU.add))
        _vc(p, lambda e: e.tensor_tensor(rs, rs, s2, ALU.mult))
    return p.cnt["vself"]


def emit_ln_norm_tile(p, sb, lb, par, ld_val, tt):
    xt, xo, sbb = lb.xt[par], lb.xln_tm[par], sb.sbb
    p.wait("vector", f"lnx{par}", ld_val)
    t_ln = p.op("vector", lambda e, a=xt, o=xo, sbb=sbb, tt=tt: e.tensor_scalar(
        o[:], a[:], sbb[:, 32 + tt:33 + tt], sbb[:, 48 + tt:49 + tt],
        ALU.subtract, ALU.mult), sem="ln_out")
    return t_ln



ARENA_F32 = 52224


class Arena:

    def __init__(self, p):
        self.t = p.ctx.enter_context(
            p.nc.sbuf_tensor("arena", [128, ARENA_F32], F32))
        self.off = 0

    def f32(self, n):
        self.off = (self.off + 7) & ~7
        v = self.t[:, self.off:self.off + n]
        self.off += n
        return v

    def bf16(self, n):
        assert n % 2 == 0
        self.off = (self.off + 7) & ~7
        v = self.t[:, self.off:self.off + n // 2].bitcast(BF16)
        self.off += n // 2
        return v

    def reset(self):
        self.off = 0


class PsumBufs:
    def __init__(self, p):
        e = p.ctx.enter_context
        self.ps1k = [e(p.nc.psum_tensor(f"ps1k{i}", [128, 1024], F32))
                     for i in range(4)]
        self.ps = [self.ps1k[i // 2][:, (i % 2) * 512:(i % 2) * 512 + 512]
                   for i in range(8)]



class FFNBufs:
    def __init__(self, p, ar, pb):
        nc, e = p.nc, p.ctx.enter_context
        ar.reset()
        self.xln_fm = ar.bf16(16 * HALF)
        self.hdn = ar.bf16(NIC * HALF)
        self.wg = [ar.bf16(16 * 128) for i in range(2)]
        self.wu = [ar.bf16(16 * 128) for i in range(2)]
        self.wdq = [ar.bf16(10 * 512) for i in range(2)]
        self.sg = [ar.bf16(HALF) for i in range(2)]
        self.xt = [ar.f32(D) for i in range(2)]
        self.xln_tm = [ar.bf16(D) for i in range(2)]
        self.dsb = ar.f32(D)
        self.xs = [ar.f32(512) for i in range(2)]
        self.res = [ar.f32(512) for i in range(2)]
        self.ev = [ar.f32(512) for i in range(2)]
        self.gs = e(nc.sbuf_tensor("gs", [128, NIC], F32))
        self.us = e(nc.sbuf_tensor("us", [128, NIC], F32))
        self.ps1k = pb.ps1k
        self.ps = pb.ps


def emit_ffn_consts(p, fb, sb, ins):
    p.dma("sync", fb.gs[:], ins["gs2"], sem="c_ld")
    p.dma("sync", fb.us[:], ins["us2"], sem="c_ld")
    dsrow = fb.xt[0][0:1, :]
    t = p.dma("sync", dsrow, ins["ds_row"], sem="c_ld")
    p.wait("tensor", "c_ld", t)
    for j in range(4):
        last = (j == 3)
        p.op("tensor", lambda e, fb=fb, sb=sb, dsrow=dsrow, j=j: e.matmul(
            fb.ps[j][:, :], sb.onesf[:], dsrow[:, j * 512:(j + 1) * 512],
            start=True, stop=True), sem="t_dsb" if last else None)
    p.wait("vector", "t_dsb", p.cnt["t_dsb"])
    for j in range(4):
        t = p.op("vector", lambda e, fb=fb, j=j: e.tensor_scalar(
            fb.dsb[:, j * 512:(j + 1) * 512], fb.ps[j][:, :], 0.5, None, ALU.mult),
            sem="v_dsb")
    return t


def emit_ffn_pass(p, fb, sb, src_x, dst_x1, W, tag, dbg=None):
    s_ldw = "ffn_ldw"
    s_tgu = "ffn_tgu"
    s_ssg = "ffn_ssg"
    s_vhd = "ffn_vhd"
    s_ldd = "ffn_ldd"
    s_tdn = "ffn_tdn"
    s_ldxs = "ffn_ldxs"
    s_vdn = "ffn_vdn"
    s_stx = "ffn_stx"
    s_ttr = "ffn_ttr"
    s_vtr = "ffn_vtr"
    for s in (s_ldw, s_tgu, s_ssg, s_vhd, s_ldd, s_tdn, s_ldxs, s_vdn, s_stx,
              s_ttr, s_vtr, "ln_out"):
        p.sem(s)

    if "t_dsb" in p.cnt:
        p.wait("sync", "t_dsb", p.cnt["t_dsb"])
    p.sem(f"{s_stx}0")
    p.sem(f"{s_stx}1")
    for h in range(2):
        if p.cnt[s_tgu] > 0:
            p.wait("vector", s_tgu, p.cnt[s_tgu])
        if p.cnt[s_vdn] > 0:
            p.wait("tensor", s_vdn, p.cnt[s_vdn])
        if "v_dsb" in p.cnt and p.cnt[s_ttr] == 0:
            p.wait("tensor", "v_dsb", p.cnt["v_dsb"])
        p.sem("ln_st")
        if p.cnt[s_ttr] > 0:
            p.wait("vector", s_ttr, p.cnt[s_ttr])
        for tt in range(8):
            par = tt % 2
            row0 = h * HALF + tt * 128
            if p.cnt["ln_st"] >= 2:
                p.wait("sync", "ln_st", p.cnt["ln_st"] - 1)
            if tt < 2 and p.cnt["ln_out"] >= 2:
                p.wait("sync", "ln_out", p.cnt["ln_out"] - 1)
            lv = p.dma("sync", fb.xt[par][:], src_x[row0:row0 + 128, :],
                       sem=f"lnx{par}")
            emit_ln_stats_tile(p, sb, fb, par, lv, tt)
        emit_rstd_batch(p, sb, 8)
        for tt in range(8):
            par = tt % 2
            row0 = h * HALF + tt * 128
            if tt < 2:
                p.wait("sync", "ln_st", p.cnt["ln_st"])
            if p.cnt["ln_out"] >= 2:
                p.wait("sync", "ln_out", p.cnt["ln_out"] - 1)
            if p.cnt[s_vtr] >= 8:
                p.wait("sync", s_vtr, p.cnt[s_vtr] - 4)
            lv = p.dma("sync", fb.xt[par][:], src_x[row0:row0 + 128, :],
                       sem=f"lnx{par}")
            t_ln = emit_ln_norm_tile(p, sb, fb, par, lv, tt)
            p.wait("tensor", "ln_out", t_ln)
            if p.cnt[s_vtr] > 0:
                p.wait("tensor", s_vtr, p.cnt[s_vtr])
            for hc in range(16):
                bank, slot = hc // 4, hc % 4
                p.op("tensor", lambda e, fb=fb, sb=sb, par=par, hc=hc, bank=bank,
                     slot=slot: e.matmul(
                         fb.ps[bank][:, slot * 128:slot * 128 + 128],
                         fb.xln_tm[par][:, hc * 128:hc * 128 + 128],
                         sb.ident[:], start=True, stop=True),
                     sem=s_ttr if hc == 15 else None)
            p.wait("vector", s_ttr, p.cnt[s_ttr])
            for bank in range(4):
                hc0 = bank * 4
                dst = fb.xln_fm[:].rearrange("p (hc t) -> p hc t", hc=16)[
                    :, hc0:hc0 + 4, tt * 128:tt * 128 + 128]
                src = fb.ps[bank][:].rearrange("p (s t) -> p s t", s=4)
                p.op("vector", lambda e, dst=dst, src=src: e.tensor_copy(dst, src),
                     sem=s_vtr)

        for ic in range(44):
            par = ic % 2
            first2 = (h == 0 and ic < 2)
            if not first2:
                p.wait("gpsimd", s_tgu, p.cnt[s_tgu] - 1)
            p.dma("gpsimd", fb.wg[par][:].rearrange("p (a b) -> p a b", a=16),
                  W["Wg_s"][ic].rearrange("(a p) i -> p a i", p=128),
                  sem=f"{s_ldw}{par}")
            lw = p.dma("gpsimd", fb.wu[par][:].rearrange("p (a b) -> p a b", a=16),
                       W["Wu_s"][ic].rearrange("(a p) i -> p a i", p=128),
                       sem=f"{s_ldw}{par}")
            p.wait("tensor", f"{s_ldw}{par}", lw)
            if ic == 0:
                p.wait("tensor", s_vtr, p.cnt[s_vtr])
            if not first2:
                p.wait("tensor", s_vhd, p.cnt[s_vhd] - 1)
            for w_i, (wt, pb) in enumerate(((fb.wg[par], fb.ps1k[par * 2]),
                                            (fb.wu[par], fb.ps1k[par * 2 + 1]))):
                for tg in range(2):
                    for hc in range(16):
                        last = (w_i == 1 and tg == 1 and hc == 15)
                        p.op("tensor", lambda e, fb=fb, wt=wt, pb=pb, tg=tg, hc=hc:
                             e.matmul(
                                 pb[:, tg * 512:tg * 512 + 512],
                                 wt[:, hc * 128:hc * 128 + 128],
                                 fb.xln_fm[:, hc * HALF + tg * 512:
                                           hc * HALF + tg * 512 + 512],
                                 start=(hc == 0), stop=(hc == 15)),
                             sem=s_tgu if last else None)
            t_gu = p.cnt[s_tgu]
            p.wait("scalar", s_tgu, t_gu)
            p.op("scalar", lambda e, fb=fb, par=par, ic=ic: e.activation(
                fb.sg[par][:], fb.ps1k[par * 2][:], AF.Silu,
                scale=fb.gs[:, ic:ic + 1]))
            t_sg = p.op("scalar", lambda e, fb=fb, par=par, sb=sb: e.copy(
                sb.flush[:], fb.sg[par][:, HALF - 1:HALF]), sem=s_ssg)
            p.wait("vector", s_ssg, t_sg)
            p.op("vector", lambda e, fb=fb, par=par, ic=ic: e.scalar_tensor_tensor(
                fb.hdn[:, ic * HALF:(ic + 1) * HALF], fb.ps1k[par * 2 + 1][:],
                fb.us[:, ic:ic + 1], fb.sg[par][:], ALU.mult, ALU.mult), sem=s_vhd)
        vhd_half = p.cnt[s_vhd]
        if dbg is not None and h == 0:
            p.wait("sync", s_vhd, vhd_half)
            p.dma("sync", dbg["hdn0"], fb.hdn[:], sem="dbg0")
            p.dma("sync", dbg["xfm0"], fb.xln_fm[:], sem="dbg0")

        G_ICS = [(0, 10), (10, 10), (20, 10), (30, 10), (40, 4)]
        for hq in range(4):
            p.wait("tensor", s_vhd, vhd_half)
            if p.cnt[s_vdn] > 0:
                p.wait("tensor", s_vdn, p.cnt[s_vdn])
            for grp, (g0, gn) in enumerate(G_ICS):
                par = (hq * len(G_ICS) + grp) % 2
                if p.cnt[s_tdn] >= 2:
                    p.wait("gpsimd", s_tdn, p.cnt[s_tdn] - 1)
                ld = p.dma("gpsimd",
                           fb.wdq[par][:, 0:gn * 512].rearrange(
                               "p (a b) -> p a b", a=gn),
                           W["Wd_s"][hq, g0:g0 + gn].rearrange(
                               "a p i -> p a i"), sem=f"{s_ldd}{par}")
                p.wait("tensor", f"{s_ldd}{par}", ld)
                for ic_l in range(gn):
                    ic = g0 + ic_l
                    for tt in range(8):
                        last = (ic_l == gn - 1 and tt == 7)
                        p.op("tensor", lambda e, fb=fb, ic=ic, ic_l=ic_l, tt=tt,
                             par=par: e.matmul(
                                 fb.ps[tt][:],
                                 fb.hdn[:, ic * HALF + tt * 128:
                                        ic * HALF + tt * 128 + 128],
                                 fb.wdq[par][:, ic_l * 512:ic_l * 512 + 512],
                                 start=(ic == 0), stop=(ic == NIC - 1)),
                             sem=s_tdn if last else None)
            t_dn = p.cnt[s_tdn]
            for tt in range(8):
                bpar = tt % 2
                row0 = h * HALF + tt * 128
                if p.cnt[s_vdn] >= 2:
                    p.wait("sync", s_vdn, p.cnt[s_vdn] - 1)
                lx = p.dma("sync", fb.xs[bpar][:],
                           src_x[row0:row0 + 128, hq * 512:hq * 512 + 512],
                           sem=f"{s_ldxs}{bpar}")
                p.wait("vector", s_tdn, t_dn)
                p.wait("vector", f"{s_ldxs}{bpar}", lx)
                p.op("vector", lambda e, fb=fb, tt=tt, hq=hq, bpar=bpar:
                     e.tensor_tensor(fb.ev[bpar][:], fb.ps[tt][:],
                                     fb.dsb[:, hq * 512:hq * 512 + 512], ALU.mult))
                if p.cnt.get(f"{s_stx}{bpar}", 0) > 0:
                    p.wait("vector", f"{s_stx}{bpar}", p.cnt[f"{s_stx}{bpar}"])
                t_vd = p.op("vector", lambda e, fb=fb, bpar=bpar: e.tensor_tensor(
                    fb.res[bpar][:], fb.ev[bpar][:], fb.xs[bpar][:], ALU.add),
                    sem=s_vdn)
                p.wait("sync", s_vdn, t_vd)
                p.dma("sync", dst_x1[row0:row0 + 128, hq * 512:hq * 512 + 512],
                      fb.res[bpar][:], sem=f"{s_stx}{bpar}")
    return p.cnt[f"{s_stx}0"] + p.cnt[f"{s_stx}1"]


def emit_ln_store_pass(p, sb, fb, src_ap, dst_bf16_ap, nrows=TOK):
    s_st = "lnp_st"
    p.sem(s_st)
    p.sem("ln_st")
    nt = nrows // 128
    assert nt <= 16
    if p.cnt.get("ffn_ttr", 0) > 0:
        p.wait("vector", "ffn_ttr", p.cnt["ffn_ttr"])
    for tt in range(nt):
        par = tt % 2
        if p.cnt["ln_st"] >= 2:
            p.wait("sync", "ln_st", p.cnt["ln_st"] - 1)
        if tt < 2 and p.cnt["ln_out"] >= 2:
            p.wait("sync", "ln_out", p.cnt["ln_out"] - 1)
        lv = p.dma("sync", fb.xt[par][:], src_ap[tt * 128:tt * 128 + 128, :],
                   sem=f"lnx{par}")
        emit_ln_stats_tile(p, sb, fb, par, lv, tt)
    emit_rstd_batch(p, sb, nt)
    for tt in range(nt):
        par = tt % 2
        if tt < 2:
            p.wait("sync", "ln_st", p.cnt["ln_st"])
        if p.cnt["ln_out"] >= 2:
            p.wait("sync", "ln_out", p.cnt["ln_out"] - 1)
        if p.cnt.get(f"{s_st}{par}", 0) > 0:
            p.wait("sync", f"{s_st}{par}", p.cnt[f"{s_st}{par}"])
        lv = p.dma("sync", fb.xt[par][:], src_ap[tt * 128:tt * 128 + 128, :],
                   sem=f"lnx{par}")
        t_ln = emit_ln_norm_tile(p, sb, fb, par, lv, tt)
        p.wait("sync", "ln_out", t_ln)
        p.dma("sync", dst_bf16_ap[tt * 128:tt * 128 + 128, :], fb.xln_tm[par][:],
              sem=f"{s_st}{par}")
    return p.cnt[f"{s_st}0"] + p.cnt[f"{s_st}1"]



def np_ln(x):
    m = x.mean(-1, keepdims=True)
    v = x.var(-1, keepdims=True)
    return (x - m) / np.sqrt(v + LN_EPS)


def np_silu(x):
    return x / (1 + np.exp(-x))


def np_ffn(x, gate_w, up_w, down_w, gate_s, up_s, down_s):
    g = x @ gate_w.T * gate_s
    u = x @ up_w.T * up_s
    h = np_silu(g) * u
    return (h @ down_w.T) * down_s



NKH_C = 4
NVH_C = 8
CHUNK = 64
BLK = 512
NBLK = SEQ_BLKS = 16


def np_attn_full(x_ln, W_qkvz, W_ba, conv_kernel, A_log, dt_bias, norm_weight):
    S = x_ln.shape[0]
    NKH, NVH, DK, DV, KSZ = 16, 32, 128, 128, 4
    qkvz = x_ln @ W_qkvz
    ba = x_ln @ W_ba
    mixed = qkvz.reshape(S, NKH, 768)
    q = mixed[:, :, 0:128]
    k = mixed[:, :, 128:256]
    v = mixed[:, :, 256:512].reshape(S, NVH, DV)
    z = mixed[:, :, 512:768].reshape(S, NVH, DV)
    mba = ba.reshape(S, NKH, 4)
    b_g = mba[:, :, 0:2].reshape(S, NVH)
    a_g = mba[:, :, 2:4].reshape(S, NVH)
    qkv = np.concatenate([q.reshape(S, -1), k.reshape(S, -1), v.reshape(S, -1)], -1)
    x = np.concatenate([np.zeros((KSZ - 1, qkv.shape[1]), qkv.dtype), qkv], 0)
    conv = np.zeros_like(qkv)
    for j in range(KSZ):
        conv += x[j:j + S, :] * conv_kernel[j, 0, :][None, :]
    qkv_c = np_silu(conv)
    q_c = qkv_c[:, 0:NKH * DK].reshape(S, NKH, DK)
    k_c = qkv_c[:, NKH * DK:2 * NKH * DK].reshape(S, NKH, DK)
    v_c = qkv_c[:, 2 * NKH * DK:].reshape(S, NVH, DV)
    beta = 1 / (1 + np.exp(-b_g))
    sp = np.logaddexp(0, a_g + dt_bias[None, :])
    g = -np.exp(A_log)[None, :] * sp
    qq = np.repeat(q_c, 2, axis=1).astype(np.float64)
    kk = np.repeat(k_c, 2, axis=1).astype(np.float64)
    vv = v_c.astype(np.float64)
    gg = g.astype(np.float64)
    bb = beta.astype(np.float64)
    nc_ = S // CHUNK
    out = np.zeros((S, NVH, DV))
    for h in range(NVH):
        state = np.zeros((DK, DV))
        for c in range(nc_):
            sl = slice(c * CHUNK, (c + 1) * CHUNK)
            qi = qq[sl, h, :] / np.sqrt(DK)
            ki = kk[sl, h, :]
            vi = vv[sl, h, :]
            gi = gg[sl, h]
            bi = bb[sl, h]
            gc = np.cumsum(gi)
            decay = np.exp(np.tril(gc[:, None] - gc[None, :]))
            kbc = ki * bi[:, None]
            L = -(kbc @ ki.T) * decay
            L = np.tril(L, -1)
            attn = np.linalg.inv(np.eye(CHUNK) - L)
            v_beta = vi * bi[:, None]
            v_i = attn @ v_beta
            kcd = attn @ (kbc * np.exp(gc)[:, None])
            a_i = np.tril((qi @ ki.T) * decay)
            v_new = v_i - kcd @ state
            o_i = (qi * np.exp(gc)[:, None]) @ state + a_i @ v_new
            g_last = gc[-1]
            state = state * np.exp(g_last) + \
                (ki * np.exp(g_last - gc)[:, None]).T @ v_new
            out[sl, h, :] = o_i
    xf = out * np_silu(z.astype(np.float64))
    var = (xf * xf).mean(-1, keepdims=True)
    xf = xf * (1 / np.sqrt(var + RMS_EPS)) * norm_weight[None, None, :]
    return xf.reshape(S, NVH * DV).astype(np.float32)


def np_attn_full_parts(x_ln, W_qkvz, W_ba, conv_kernel, A_log, dt_bias,
                       norm_weight):
    S = x_ln.shape[0]
    NKH, NVH, DK, DV, KSZ = 16, 32, 128, 128, 4
    qkvz = x_ln @ W_qkvz
    ba = x_ln @ W_ba
    mixed = qkvz.reshape(S, NKH, 768)
    q = mixed[:, :, 0:128]
    k = mixed[:, :, 128:256]
    v = mixed[:, :, 256:512].reshape(S, NVH, DV)
    z = mixed[:, :, 512:768].reshape(S, NVH, DV)
    mba = ba.reshape(S, NKH, 4)
    b_g = mba[:, :, 0:2].reshape(S, NVH)
    a_g = mba[:, :, 2:4].reshape(S, NVH)
    qkv = np.concatenate([q.reshape(S, -1), k.reshape(S, -1), v.reshape(S, -1)], -1)
    x = np.concatenate([np.zeros((KSZ - 1, qkv.shape[1]), qkv.dtype), qkv], 0)
    conv = np.zeros_like(qkv)
    for j in range(KSZ):
        conv += x[j:j + S, :] * conv_kernel[j, 0, :][None, :]
    qkv_c = np_silu(conv)
    q_c = qkv_c[:, 0:NKH * DK].reshape(S, NKH, DK)
    k_c = qkv_c[:, NKH * DK:2 * NKH * DK].reshape(S, NKH, DK)
    v_c = qkv_c[:, 2 * NKH * DK:].reshape(S, NVH, DV)
    beta = 1 / (1 + np.exp(-b_g))
    sp = np.logaddexp(0, a_g + dt_bias[None, :])
    g = -np.exp(A_log)[None, :] * sp
    qq = np.repeat(q_c, 2, axis=1).astype(np.float64)
    kk = np.repeat(k_c, 2, axis=1).astype(np.float64)
    vv = v_c.astype(np.float64)
    gg = g.astype(np.float64)
    bb = beta.astype(np.float64)
    nc_ = S // CHUNK
    out = np.zeros((S, NVH, DV))
    states_at = {}
    for h in range(NVH):
        state = np.zeros((DK, DV))
        for c in range(nc_):
            sl = slice(c * CHUNK, (c + 1) * CHUNK)
            qi = qq[sl, h, :] / np.sqrt(DK)
            ki = kk[sl, h, :]
            vi = vv[sl, h, :]
            gi = gg[sl, h]
            bi = bb[sl, h]
            gc = np.cumsum(gi)
            decay = np.exp(np.tril(gc[:, None] - gc[None, :]))
            kbc = ki * bi[:, None]
            L = -(kbc @ ki.T) * decay
            L = np.tril(L, -1)
            attn = np.linalg.inv(np.eye(CHUNK) - L)
            v_beta = vi * bi[:, None]
            v_i = attn @ v_beta
            kcd = attn @ (kbc * np.exp(gc)[:, None])
            a_i = np.tril((qi @ ki.T) * decay)
            v_new = v_i - kcd @ state
            o_i = (qi * np.exp(gc)[:, None]) @ state + a_i @ v_new
            g_last = gc[-1]
            state = state * np.exp(g_last) + \
                (ki * np.exp(g_last - gc)[:, None]).T @ v_new
            out[sl, h, :] = o_i
            end = (c + 1) * CHUNK
            if end % 1024 == 0:
                states_at.setdefault(end, {})[h] = state.copy()
    return dict(q_c=q_c, k_c=k_c, v_c=v_c, z=z, g=g, beta=beta, out=out,
                states_at=states_at)


def host_attn_weights(core, W_qkvz, W_ba, conv_kernel, A_log, dt_bias):
    kh0 = (core % 4) * 4
    khs = list(range(kh0, kh0 + 4))
    vhs = [2 * k + r for k in khs for r in range(2)]
    qcols = [768 * k + d for k in khs for d in range(128)]
    kcols = [768 * k + 128 + d for k in khs for d in range(128)]
    vcols = [768 * k + 256 + r * 128 + d for k in khs for r in range(2)
             for d in range(128)]
    zcols = [768 * k + 512 + r * 128 + d for k in khs for r in range(2)
             for d in range(128)]
    bacols = [4 * k + r for k in khs for r in range(2)] + \
             [4 * k + 2 + r for k in khs for r in range(2)]
    Wqkv = W_qkvz[:, qcols + kcols + vcols]
    Wqkv_s = np.ascontiguousarray(Wqkv.reshape(D, 16, 128).transpose(1, 0, 2))
    Wz_s = np.ascontiguousarray(W_qkvz[:, zcols].reshape(16, 128, len(zcols)))
    Wba_c = np.ascontiguousarray(W_ba[:, bacols])
    chans = ([k * 128 + d for k in khs for d in range(128)] +
             [2048 + k * 128 + d for k in khs for d in range(128)] +
             [4096 + vh * 128 + d for vh in vhs for d in range(128)])
    convw = np.ascontiguousarray(
        conv_kernel[:, 0, chans].T.reshape(16, 128, 4))
    return dict(Wqkv_s=Wqkv_s, Wz_s=Wz_s, Wba_c=Wba_c, convw=convw,
                A_c=np.ascontiguousarray(A_log[vhs].reshape(1, 8)),
                dt_c=np.ascontiguousarray(dt_bias[vhs].reshape(1, 8)))


def host_attn_consts():
    sel = np.zeros((16, 16, 128), np.float32)
    for i in range(16):
        sel[i, i, :] = 1.0
    sel64 = np.zeros((128, 128), np.float32)
    sel64[63, 0:64] = 1.0
    sel64[127, 64:128] = 1.0
    u1 = np.zeros((128, 128), np.float32)
    for b0 in (0, 64):
        for i in range(64):
            u1[b0:b0 + i + 1, b0 + i] = 1.0
    low_s = np.zeros((128, 128), np.float32)
    up_s = np.zeros((128, 128), np.float32)
    up_i = np.zeros((128, 128), np.float32)
    for b0 in (0, 64):
        for i in range(64):
            for j in range(64):
                ii, jj = b0 + i, b0 + j
                if i > j:
                    low_s[ii, jj] = 1.0
                    up_s[jj, ii] = 1.0
                if i >= j:
                    up_i[jj, ii] = 1.0 / np.sqrt(128.0)
    t4 = lambda m: np.ascontiguousarray(np.tile(m, (1, 4)))
    return dict(SEL=sel, SEL64=sel64, U1BD=u1,
                MLOW4=t4(low_s).astype(bf), MUPS4=t4(up_s).astype(bf),
                MUPI4=t4(up_i).astype(bf))


class AttnBufs:

    def __init__(self, p, ar, pb):
        nc, e = p.nc, p.ctx.enter_context
        ar.reset()
        self.q_dm = ar.bf16(4 * 1024)
        self.k_dm = ar.bf16(4 * 1024)
        self.v_dm = ar.bf16(8 * 1024)
        self.z_act = ar.bf16(8 * 1024)
        self.outb = ar.bf16(8 * 1024)
        self.state = ar.f32(8 * 128)
        self.state_bf = ar.bf16(1024)
        self.tab = ar.f32(768)
        self.elb = ar.f32(8)
        self.epiv = ar.f32(64)
        self.wba = ar.bf16(16 * 16)
        self.tailb = ar.bf16(48)
        self.nwbc = ar.f32(128)
        self.negA = ar.f32(8)
        self.dtb = ar.f32(8)
        self.cdiag = ar.bf16(64 * 128)
        a0 = ar.off
        self.x_fm = ar.bf16(16 * 1024)
        self.wq = [ar.bf16(16 * 128) for _ in range(2)]
        self.wz = [ar.bf16(1024) for _ in range(2)]
        self.raw = ar.bf16(16 * 1032)
        proj_end = ar.off
        ar.off = a0
        self.E1m = ar.f32(1024)
        self.rec = ar.f32(1024)
        self.D1 = ar.bf16(1024)
        self.D1T = ar.bf16(1024)
        self.D2 = ar.bf16(1024)
        self.Dst = ar.bf16(1024)
        self.A = [ar.bf16(1024) for _ in range(6)]
        self.Ppp = [ar.bf16(1024) for _ in range(2)]
        self.Qpp = [ar.bf16(1024) for _ in range(2)]
        self.aT = ar.bf16(1024)
        self.kTM = ar.bf16(512)
        self.vbc = ar.bf16(1024)
        self.kmod = ar.bf16(1024)
        self.u = [ar.bf16(1024) for _ in range(2)]
        self.vnew = ar.bf16(1024)
        self.tsb = ar.f32(1024)
        delta_end = ar.off
        ar.off = max(proj_end, delta_end)
        self.rows = ar.f32(128)[0:16, :]
        self.sel = ar.f32(2048)[0:16, :]
        self.sel64 = ar.f32(128)
        self.u1bd = ar.f32(128)
        self.mlow4 = ar.bf16(512)
        self.mups4 = ar.bf16(512)
        self.mupi4 = ar.bf16(512)
        self.identf = ar.f32(128)
        self.ps1k = pb.ps1k
        self.ps = pb.ps


def emit_attn_consts(p, ab, sb, ins):
    cl = "ac_ld"
    tg_ = p.dma("gpsimd", ab.wba[:].rearrange("p (a b) -> p a b", a=16),
                ins["Wba_c"].rearrange("(a p) i -> p a i", p=128), sem="ac_gld")
    cw = ab.tsb[:, 0:64]
    p.dma("sync", cw.rearrange("p (a b) -> p a b", a=16),
          ins["convw"].rearrange("a p b -> p a b"), sem=cl)
    p.dma("sync", ab.negA[0:1, :], ins["A_c"], sem=cl)
    p.dma("sync", ab.dtb[0:1, :], ins["dt_c"], sem=cl)
    t = p.dma("sync", ab.nwbc[0:1, 0:128],
              ins["norm_weight"].rearrange("(a b) -> a b", a=1), sem=cl)
    p.dma("sync", ab.sel.rearrange("p (a b) -> p a b", a=16),
          ins["SEL"].rearrange("a p b -> p a b"), sem=cl)
    p.dma("sync", ab.sel64, ins["SEL64"], sem=cl)
    p.dma("sync", ab.u1bd, ins["U1BD"], sem=cl)
    p.dma("sync", ab.mlow4, ins["MLOW4"], sem=cl)
    p.dma("sync", ab.mups4, ins["MUPS4"], sem=cl)
    p.dma("sync", ab.mupi4, ins["MUPI4"], sem=cl)
    t = p.dma("sync", ab.identf, ins["IDENTF"], sem=cl)
    p.wait("tensor", cl, t)
    p.wait("tensor", "ac_gld", tg_)
    p.wait("tensor", "c_ld", p.cnt["c_ld"])
    p.wait("vector", "c_ld", p.cnt["c_ld"])
    p.wait("scalar", cl, t)
    t_e = p.op("scalar", lambda e: e.activation(ab.negA[0:1, :], ab.negA[0:1, :],
                                                AF.Exp), sem="ac_s")
    p.wait("vector", "ac_s", t_e)
    t_n = p.op("vector", lambda e: e.tensor_scalar(
        ab.negA[0:1, :], ab.negA[0:1, :], -1.0, None, ALU.mult), sem="ac_v")
    p.wait("tensor", "ac_v", t_n)
    p.op("tensor", lambda e: e.matmul(ab.ps[0][:, 0:8], sb.onesf[:],
                                      ab.negA[0:1, :], start=True, stop=True))
    p.op("tensor", lambda e: e.matmul(ab.ps[0][:, 8:16], sb.onesf[:],
                                      ab.dtb[0:1, :], start=True, stop=True))
    t_b = p.op("tensor", lambda e: e.matmul(ab.ps[1][:, 0:128], sb.onesf[:],
                                            ab.nwbc[0:1, 0:128],
                                            start=True, stop=True), sem="ac_t")
    p.wait("vector", "ac_t", t_b)
    p.op("vector", lambda e: e.tensor_copy(ab.negA[:], ab.ps[0][:, 0:8]))
    p.op("vector", lambda e: e.tensor_copy(ab.dtb[:], ab.ps[0][:, 8:16]))
    t_c = p.op("vector", lambda e: e.tensor_copy(ab.nwbc[:], ab.ps[1][:, 0:128]),
               sem="ac_v")
    p.wait("vector", cl, p.cnt[cl])
    for cb in range(16):
        for j in range(4):
            t_c = p.op("vector", lambda e, cb=cb, j=j: e.tensor_tensor(
                ab.cdiag[:, (cb * 4 + j) * 128:(cb * 4 + j) * 128 + 128],
                sb.ident[:],
                cw[:, cb * 4 + j:cb * 4 + j + 1].broadcast_to([128, 128]),
                ALU.mult), sem="ac_v")
    return t_c


def _sc(p, fn):
    t = p.op("scalar", fn, sem="sself")
    p.wait("scalar", "sself", t)
    return t


TB_BETA, TB_GCUM, TB_NEGB, TB_EG, TB_SK, TB_SU, TB_SQ, TB_ELB = \
    0, 64, 128, 192, 256, 320, 384, 448
TB_S, TB_T1, TB_G, TB_T3 = 512, 576, 640, 704


def emit_attn_precast(p, ab, ins, aps):
    for cb in range(16):
        par = cb % 2
        if cb >= 2:
            p.wait("gpsimd", f"pc_st{par}", p.cnt[f"pc_st{par}"])
        lw = p.dma("gpsimd", ab.wq[par][:].rearrange("p (a b) -> p a b", a=16),
                   ins["Wqkv_s"][cb].rearrange("(a p) i -> p a i", p=128),
                   sem=f"pc_ld{par}")
        p.wait("sync", f"pc_ld{par}", lw)
        p.dma("sync", aps["Wqkv_bf"][cb].rearrange("(a p) i -> p a i", p=128),
              ab.wq[par][:].rearrange("p (a b) -> p a b", a=16),
              sem=f"pc_st{par}")
    for hc in range(16):
        par = hc % 2
        p.wait("gpsimd", f"pc_st{par}", p.cnt[f"pc_st{par}"])
        lw = p.dma("gpsimd", ab.wz[par][:], ins["Wz_s"][hc], sem=f"pc_ld{par}")
        p.wait("sync", f"pc_ld{par}", lw)
        p.dma("sync", aps["Wz_bf"][hc], ab.wz[par][:], sem=f"pc_st{par}")
    return p.cnt["pc_st0"] + p.cnt["pc_st1"]


def emit_attn_tables(p, ab, sb, hf, xf_ld):
    tb = ab.tab
    p.sem("at_t")
    p.wait("tensor", "xf_ld", xf_ld)
    for sname in ("ap_s", "dl_v", "dl_s"):
        if p.cnt.get(sname, 0) > 0:
            p.wait("tensor", sname, p.cnt[sname])
    if p.cnt.get("dl_t", 0) > 0:
        p.wait("vector", "dl_t", p.cnt["dl_t"])
        p.wait("scalar", "dl_t", p.cnt["dl_t"])
        p.wait("vector", "dl_v", p.cnt["dl_v"])
    for tt in range(8):
        for hc in range(16):
            p.op("tensor", lambda e, tt=tt, hc=hc: e.matmul(
                ab.ps[0][:, tt * 16:tt * 16 + 16],
                ab.x_fm[:, hc * 1024 + tt * 128:hc * 1024 + tt * 128 + 128],
                ab.wba[:, hc * 16:hc * 16 + 16],
                start=(hc == 0), stop=(hc == 15)),
                sem="at_t" if (tt == 7 and hc == 15) else None)
    t_ba = p.cnt["at_t"]
    p.wait("scalar", "at_t", t_ba)
    bview = ab.ps[0][:, 0:128].rearrange("p (t c) -> p t c", c=16)[:, :, 0:8]
    aview = ab.ps[0][:, 0:128].rearrange("p (t c) -> p t c", c=16)[:, :, 8:16]
    _sc(p, lambda e: e.activation(
        tb[:, TB_BETA:TB_BETA + 64].rearrange("p (t c) -> p t c", c=8),
        bview, AF.Sigmoid))
    p.wait("vector", "sself", p.cnt["sself"])
    _vc(p, lambda e: e.tensor_scalar(
        tb[:, TB_NEGB:TB_NEGB + 64], tb[:, TB_BETA:TB_BETA + 64],
        -1.0, None, ALU.mult))
    _vc(p, lambda e: e.tensor_tensor(
        tb[:, TB_S:TB_S + 64].rearrange("p (t c) -> p t c", c=8), aview,
        ab.dtb[:].unsqueeze(1).broadcast_to([128, 8, 8]), ALU.add))
    p.wait("scalar", "vself", p.cnt["vself"])
    _sc(p, lambda e: e.activation(tb[:, TB_T1:TB_T1 + 64], tb[:, TB_S:TB_S + 64],
                                  AF.Abs))
    _sc(p, lambda e: e.activation(tb[:, TB_T3:TB_T3 + 64], tb[:, TB_T1:TB_T1 + 64],
                                  AF.Exp, scale=sb.kc[:, 4:5]))
    _sc(p, lambda e: e.activation(tb[:, TB_T1:TB_T1 + 64], tb[:, TB_T3:TB_T3 + 64],
                                  AF.Ln, bias=sb.kc[:, 5:6]))
    _sc(p, lambda e: e.activation(tb[:, TB_T3:TB_T3 + 64], tb[:, TB_S:TB_S + 64],
                                  AF.Relu))
    p.wait("vector", "sself", p.cnt["sself"])
    _vc(p, lambda e: e.tensor_tensor(tb[:, TB_S:TB_S + 64], tb[:, TB_T1:TB_T1 + 64],
                                     tb[:, TB_T3:TB_T3 + 64], ALU.add))
    _vc(p, lambda e: e.tensor_tensor(
        tb[:, TB_G:TB_G + 64].rearrange("p (t c) -> p t c", c=8),
        tb[:, TB_S:TB_S + 64].rearrange("p (t c) -> p t c", c=8),
        ab.negA[:].unsqueeze(1).broadcast_to([128, 8, 8]), ALU.mult))
    p.wait("tensor", "vself", p.cnt["vself"])
    for tt in range(8):
        p.op("tensor", lambda e, tt=tt: e.matmul(
            ab.ps[2][:, tt * 8:tt * 8 + 8], ab.u1bd[:],
            tb[:, TB_G + tt * 8:TB_G + tt * 8 + 8], start=True, stop=True),
            sem="at_t" if tt == 7 else None)
    t_gc = p.cnt["at_t"]
    p.wait("vector", "at_t", t_gc)
    _vc(p, lambda e: e.tensor_copy(tb[:, TB_GCUM:TB_GCUM + 64], ab.ps[2][:, 0:64]))
    p.wait("tensor", "vself", p.cnt["vself"])
    for tt in range(8):
        p.op("tensor", lambda e, tt=tt: e.matmul(
            ab.ps[4][:, tt * 8:tt * 8 + 8], ab.sel64[:],
            tb[:, TB_GCUM + tt * 8:TB_GCUM + tt * 8 + 8], start=True, stop=True),
            sem="at_t" if tt == 7 else None)
    t_gl = p.cnt["at_t"]
    p.wait("vector", "at_t", t_gl)
    _vc(p, lambda e: e.tensor_tensor(tb[:, TB_T3:TB_T3 + 64], ab.ps[4][:, 0:64],
                                     tb[:, TB_GCUM:TB_GCUM + 64], ALU.subtract))
    p.wait("scalar", "vself", p.cnt["vself"])
    _sc(p, lambda e: e.activation(tb[:, TB_EG:TB_EG + 64],
                                  tb[:, TB_GCUM:TB_GCUM + 64], AF.Exp))
    _sc(p, lambda e: e.activation(tb[:, TB_SK:TB_SK + 64],
                                  tb[:, TB_T3:TB_T3 + 64], AF.Exp))
    p.wait("vector", "sself", p.cnt["sself"])
    _vc(p, lambda e: e.tensor_tensor(tb[:, TB_SU:TB_SU + 64],
                                     tb[:, TB_BETA:TB_BETA + 64],
                                     tb[:, TB_EG:TB_EG + 64], ALU.mult))
    _vc(p, lambda e: e.tensor_scalar(tb[:, TB_SQ:TB_SQ + 64],
                                     tb[:, TB_EG:TB_EG + 64],
                                     float(1.0 / np.sqrt(128.0)), None, ALU.mult))
    _vc(p, lambda e: e.tensor_tensor(tb[:, TB_ELB:TB_ELB + 64],
                                     tb[:, TB_SK:TB_SK + 64],
                                     tb[:, TB_EG:TB_EG + 64], ALU.mult))
    return p.cnt["vself"]


PROJ_PARTS = "qcdtz"


def emit_attn_proj(p, ab, sb, aps, hf):
    sq = "ap_ldq"
    st_ = "ap_t"
    sv = "ap_v"
    ss = "ap_s"
    for x in (sq, st_, sv, ss):
        p.sem(x)
    first = p.cnt[st_] == 0
    p.wait("tensor", "sself", p.cnt["sself"])
    p.wait("tensor", "vself", p.cnt["vself"])
    if p.cnt.get("dl_t", 0) > 0:
        p.wait("scalar", "dl_t", p.cnt["dl_t"])
        p.wait("scalar", "dl_v", p.cnt["dl_v"])
        p.wait("tensor", "dl_v", p.cnt["dl_v"])
    if p.cnt.get("ep_st", 0) > 0:
        p.wait("scalar", "ep_st", p.cnt["ep_st"])
        p.wait("vector", "ep_st", p.cnt["ep_st"])
    if p.cnt[ss] > 0:
        p.wait("tensor", ss, p.cnt[ss])
        p.wait("scalar", sv, p.cnt[sv])
    for par_ in range(2):
        if p.cnt.get(f"pc_st{par_}", 0) > 0:
            p.wait("sync", f"pc_st{par_}", p.cnt[f"pc_st{par_}"])
    if hf > 0 and "t" in PROJ_PARTS:
        p.wait("vector", "dl_t", p.cnt["dl_t"])
        p.wait("vector", "dl_s", p.cnt["dl_s"])
        if p.cnt.get("ep_st", 0) > 0:
            p.wait("vector", "ep_st", p.cnt["ep_st"])
    for cb in range(16):
        par = cb % 2
        if p.cnt[ss] >= 8:
            p.wait("sync", ss, p.cnt[ss] - 6)
        lw = p.dma("sync", ab.wq[par][:].rearrange("p (a b) -> p a b", a=16),
                   aps["Wqkv_bf"][cb].rearrange("(a p) i -> p a i", p=128),
                   sem=f"{sq}{par}")
        p.wait("tensor", f"{sq}{par}", lw)
        for tg in range(2):
            pb_ = ab.ps[cb % 2]
            if tg == 1:
                p.wait("tensor", ss, p.cnt[ss])
            elif p.cnt[ss] >= 8:
                p.wait("tensor", ss, p.cnt[ss] - 6)
            for hc in range(16):
                p.op("tensor", lambda e, par=par, hc=hc, tg=tg, pb_=pb_: e.matmul(
                    pb_[:], ab.wq[par][:, hc * 128:hc * 128 + 128],
                    ab.x_fm[:, hc * 1024 + tg * 512:hc * 1024 + tg * 512 + 512],
                    start=(hc == 0), stop=(hc == 15)),
                    sem=st_ if hc == 15 else None)
            t_q = p.cnt[st_]
            p.wait("scalar", st_, t_q)
            p.op("scalar", lambda e, cb=cb, tg=tg, pb_=pb_: e.copy(
                ab.raw[:, cb * 1032 + 8 + tg * 512:cb * 1032 + 8 + tg * 512 + 512],
                pb_[:]), sem=ss)
        if "c" not in PROJ_PARTS:
            continue
        if hf > 0 and "t" in PROJ_PARTS:
            p.op("vector", lambda e, cb=cb: e.tensor_copy(
                ab.raw[:, cb * 1032 + 5:cb * 1032 + 8],
                ab.tailb[:, cb * 3:cb * 3 + 3]), sem=sv)
            p.wait("tensor", sv, p.cnt[sv])
        p.wait("tensor", ss, p.cnt[ss])
        for tg in range(2):
            pc = ab.ps[2 + cb % 2]
            if p.cnt[ss] > 0:
                p.wait("tensor", ss, p.cnt[ss])
            for j in range(4):
                p.op("tensor", lambda e, cb=cb, tg=tg, j=j, pc=pc: e.matmul(
                    pc[:], ab.cdiag[:, (cb * 4 + j) * 128:(cb * 4 + j) * 128 + 128],
                    ab.raw[:, cb * 1032 + 5 + tg * 512 + j:
                           cb * 1032 + 5 + tg * 512 + j + 512],
                    start=(j == 0), stop=(j == 3)),
                    sem=st_ if j == 3 else None)
            t_c = p.cnt[st_]
            if "d" not in PROJ_PARTS:
                continue
            p.wait("scalar", st_, t_c)
            if cb < 4:
                dst = ab.q_dm[:, cb * 1024 + tg * 512:cb * 1024 + tg * 512 + 512]
            elif cb < 8:
                dst = ab.k_dm[:, (cb - 4) * 1024 + tg * 512:
                              (cb - 4) * 1024 + tg * 512 + 512]
            else:
                dst = ab.v_dm[:, (cb - 8) * 1024 + tg * 512:
                              (cb - 8) * 1024 + tg * 512 + 512]
            _af = AF.Silu
            p.op("scalar", lambda e, dst=dst, pc=pc, _af=_af: e.activation(
                dst, pc[:], _af), sem=ss)
        if "t" in PROJ_PARTS:
            p.wait("vector", st_, p.cnt[st_])
            p.wait("vector", ss, p.cnt[ss])
            p.op("vector", lambda e, cb=cb: e.tensor_copy(
                ab.tailb[:, cb * 3:cb * 3 + 3],
                ab.raw[:, cb * 1032 + 1029:cb * 1032 + 1032]), sem=sv)
    if "z" not in PROJ_PARTS:
        return p.cnt[ss], p.cnt[sv]
    for zh in range(2):
        for hc in range(16):
            par = hc % 2
            if hc >= 2 or zh == 1:
                p.wait("sync", st_, p.cnt[st_] - 1)
            lw = p.dma("sync", ab.wz[par][:, 0:512],
                       aps["Wz_bf"][hc][:, zh * 512:zh * 512 + 512],
                       sem=f"ap_ldz{par}")
            p.wait("tensor", f"ap_ldz{par}", lw)
            if hc == 0:
                p.wait("tensor", ss, p.cnt[ss])
            for tt in range(8):
                p.op("tensor", lambda e, par=par, hc=hc, tt=tt: e.matmul(
                    ab.ps[tt][:], ab.x_fm[:, hc * 1024 + tt * 128:
                                          hc * 1024 + tt * 128 + 128],
                    ab.wz[par][:, 0:512],
                    start=(hc == 0), stop=(hc == 15)),
                    sem=st_ if tt == 7 else None)
        t_z = p.cnt[st_]
        p.wait("scalar", st_, t_z)
        for bb in range(4):
            zv = ab.z_act[:].rearrange("p (t c) -> p t c", c=1024)[
                :, 2 * bb:2 * bb + 2, zh * 512:zh * 512 + 512]
            pv = ab.ps1k[bb][:].rearrange("p (t c) -> p t c", c=512)
            p.op("scalar", lambda e, zv=zv, pv=pv: e.activation(
                zv, pv, AF.Silu), sem=ss)
    return p.cnt[ss], p.cnt[sv]


def _fv(p, fn):
    t = p.op("vector", fn, sem="dl_v")
    p.wait("vector", "dl_v", t)
    return t


def emit_attn_delta(p, ab, sb, hf):
    DT, DV_, DS = "dl_t", "dl_v", "dl_s"
    for x in (DT, DV_, DS):
        p.sem(x)
    tb = ab.tab
    if p.cnt.get("ap_s", 0) > 0:
        p.wait("tensor", "ap_s", p.cnt["ap_s"])
    if p.cnt.get("ap_v", 0) > 0:
        p.wait("tensor", "ap_v", p.cnt["ap_v"])

    def v3(dst2d):
        return dst2d.rearrange("p (a b) -> p a b", b=128)

    def v4(dst2d):
        return dst2d.rearrange("p (a r b) -> p a r b", r=2, b=128)

    def v4c(dst2d):
        return dst2d.rearrange("p (c a b) -> p c a b", c=2, b=128)

    def m4(mask):
        return mask.rearrange("p (a b) -> p a b", b=128) \
            .unsqueeze(1).broadcast_to([128, 2, 4, 128])

    iv3 = sb.ident[:].unsqueeze(1).broadcast_to([128, 4, 128])

    for pr in range(8):
        c2 = pr * 128
        tcol = pr * 8
        p.wait("tensor", DV_, p.cnt[DV_])
        p.wait("tensor", DS, p.cnt[DS])
        for kh in range(4):
            p.op("tensor", lambda e, kh=kh, c2=c2: e.matmul(
                ab.ps[6][:, kh * 128:kh * 128 + 128],
                ab.k_dm[:, kh * 1024 + c2:kh * 1024 + c2 + 128],
                sb.ident[:], start=True, stop=True),
                sem=DT if kh == 3 else None)
        mt_ktm = p.cnt[DT]
        for vh in range(8):
            p.op("tensor", lambda e, vh=vh, c2=c2: e.matmul(
                ab.ps1k[2][:, vh * 128:vh * 128 + 128],
                ab.v_dm[:, vh * 1024 + c2:vh * 1024 + c2 + 128],
                sb.ident[:], start=True, stop=True),
                sem=DT if vh == 7 else None)
        mt_vtm = p.cnt[DT]
        for kh in range(4):
            p.op("tensor", lambda e, kh=kh, c2=c2: e.matmul(
                ab.ps[0][:, kh * 128:kh * 128 + 128],
                ab.k_dm[:, kh * 1024 + c2:kh * 1024 + c2 + 128],
                ab.k_dm[:, kh * 1024 + c2:kh * 1024 + c2 + 128],
                start=True, stop=True), sem=DT if kh == 3 else None)
        mt_sym = p.cnt[DT]
        for kh in range(4):
            p.op("tensor", lambda e, kh=kh, c2=c2: e.matmul(
                ab.ps[1][:, kh * 128:kh * 128 + 128],
                ab.k_dm[:, kh * 1024 + c2:kh * 1024 + c2 + 128],
                ab.q_dm[:, kh * 1024 + c2:kh * 1024 + c2 + 128],
                start=True, stop=True), sem=DT if kh == 3 else None)
        mt_symq = p.cnt[DT]
        pkv = tb[:, TB_GCUM + tcol:TB_GCUM + tcol + 72] \
            .rearrange("p (a b) -> p a b", b=8)[:, 0:9:8, :]
        pkb = tb[:, TB_T3:TB_T3 + 16]
        t_c = _fv(p, lambda e, pkv=pkv, pkb=pkb: e.tensor_copy(
            pkb.rearrange("p (a b) -> p a b", b=8), pkv))
        p.wait("tensor", DV_, t_c)
        t_pk = p.op("tensor", lambda e, pkb=pkb: e.matmul(
            ab.ps[7][0:16, 0:128], pkb, ab.identf, start=True, stop=True),
            sem=DT)
        p.wait("vector", DT, t_pk)
        t_rows = _fv(p, lambda e: e.tensor_copy(
            ab.rows[:], ab.ps[7][0:16, 0:128]))
        p.wait("tensor", DV_, t_rows)
        for vh in range(8):
            p.op("tensor", lambda e, vh=vh: e.matmul(
                ab.ps1k[1][:, vh * 128:vh * 128 + 128],
                ab.sel[:, vh * 128:vh * 128 + 128], ab.rows[:],
                start=True, stop=True), sem=DT if vh == 7 else None)
        mt_R = p.cnt[DT]
        p.wait("vector", DT, mt_vtm)
        bv = tb[:, TB_BETA + tcol:TB_BETA + tcol + 8] \
            .unsqueeze(2).broadcast_to([128, 8, 128])
        t_vbc = _fv(p, lambda e, bv=bv: e.tensor_tensor(
            v3(ab.vbc[:]), v3(ab.ps1k[2][:]), bv, ALU.mult))
        p.wait("tensor", DV_, t_vbc)
        for vh in range(8):
            p.op("tensor", lambda e, vh=vh: e.matmul(
                ab.ps1k[2][:, vh * 128:vh * 128 + 128],
                ab.sel[:, 1024 + vh * 128:1024 + vh * 128 + 128], ab.rows[:],
                start=True, stop=True), sem=DT if vh == 7 else None)
        mt_Bj = p.cnt[DT]
        p.wait("vector", DT, mt_ktm)
        t_ktm = _fv(p, lambda e: e.tensor_copy(
            ab.kTM[:], ab.ps[6][:, 0:512]))
        p.wait("vector", DT, mt_R)
        giv = tb[:, TB_GCUM + tcol:TB_GCUM + tcol + 8] \
            .unsqueeze(2).broadcast_to([128, 8, 128])
        t = _fv(p, lambda e, giv=giv: e.tensor_tensor(
            v3(ab.E1m[:]), v3(ab.ps1k[1][:]), giv, ALU.subtract))
        p.wait("scalar", DV_, t)
        t = p.op("scalar", lambda e: e.activation(
            ab.rec[:], ab.E1m[:], AF.Abs), sem=DS)
        p.wait("scalar", DS, t)
        t = p.op("scalar", lambda e, sb=sb: e.activation(
            ab.E1m[:], ab.rec[:], AF.Exp, scale=sb.kc[:, 4:5]), sem=DS)
        p.wait("vector", DS, t)
        _fv(p, lambda e: e.tensor_tensor(
            v4c(ab.D2[:]), v4c(ab.E1m[:]), m4(ab.mupi4), ALU.mult))
        nbv = tb[:, TB_NEGB + tcol:TB_NEGB + tcol + 8] \
            .unsqueeze(2).broadcast_to([128, 8, 128])
        _fv(p, lambda e, nbv=nbv: e.tensor_tensor(
            v3(ab.Dst[:]), v3(ab.E1m[:]), nbv, ALU.mult))
        t = _fv(p, lambda e: e.tensor_tensor(
            v4c(ab.D1[:]), v4c(ab.Dst[:]), m4(ab.mlow4), ALU.mult))
        p.wait("vector", DT, mt_Bj)
        p.wait("vector", DV_, t)
        _fv(p, lambda e: e.tensor_tensor(
            v3(ab.Dst[:]), v3(ab.E1m[:]), v3(ab.ps1k[2][:]), ALU.mult))
        t = _fv(p, lambda e: e.tensor_tensor(
            v4c(ab.D1T[:]), v4c(ab.Dst[:]), m4(ab.mups4), ALU.mult))
        symv = ab.ps[0][:].rearrange("p (a b) -> p a b", b=128) \
            .unsqueeze(2).broadcast_to([128, 4, 2, 128])
        symqv = ab.ps[1][:].rearrange("p (a b) -> p a b", b=128) \
            .unsqueeze(2).broadcast_to([128, 4, 2, 128])
        p.wait("vector", DT, mt_symq)
        _fv(p, lambda e, symv=symv: e.tensor_tensor(
            v4(ab.Ppp[0][:]), symv, v4(ab.D1[:]), ALU.mult))
        _fv(p, lambda e, symv=symv: e.tensor_tensor(
            v4(ab.Qpp[0][:]), symv, v4(ab.D1T[:]), ALU.mult))
        t = _fv(p, lambda e, symqv=symqv: e.tensor_tensor(
            v4(ab.aT[:]), symqv, v4(ab.D2[:]), ALU.mult))
        p.wait("vector", DV_, t)
        t = _fv(p, lambda e: e.tensor_tensor(
            v4c(ab.A[0][:]), v4c(ab.Qpp[0][:]),
            iv3.unsqueeze(1).broadcast_to([128, 2, 4, 128]), ALU.add))
        for lvl in range(3):
            pa, pb2 = ab.Ppp[lvl % 2], ab.Ppp[(lvl + 1) % 2]
            qa, qb = ab.Qpp[lvl % 2], ab.Qpp[(lvl + 1) % 2]
            p.wait("tensor", DV_, p.cnt[DV_])
            for vh in range(8):
                p.op("tensor", lambda e, vh=vh, pa=pa, qa=qa: e.matmul(
                    ab.ps1k[0][:, vh * 128:vh * 128 + 128],
                    qa[:, vh * 128:vh * 128 + 128],
                    pa[:, vh * 128:vh * 128 + 128],
                    start=True, stop=True), sem=DT if vh == 7 else None)
            for vh in range(8):
                p.op("tensor", lambda e, vh=vh, pa=pa, qa=qa: e.matmul(
                    ab.ps1k[1][:, vh * 128:vh * 128 + 128],
                    pa[:, vh * 128:vh * 128 + 128],
                    qa[:, vh * 128:vh * 128 + 128],
                    start=True, stop=True), sem=DT if vh == 7 else None)
            mt_pq = p.cnt[DT]
            p.wait("vector", DT, mt_pq)
            _fv(p, lambda e, pb2=pb2: e.tensor_copy(
                pb2[:], ab.ps1k[0][:]))
            _fv(p, lambda e, qb=qb: e.tensor_copy(
                qb[:], ab.ps1k[1][:]))
            t = _fv(p, lambda e, lvl=lvl: e.tensor_tensor(
                v4c(ab.A[lvl + 1][:]), v4c(ab.ps1k[1][:]),
                iv3.unsqueeze(1).broadcast_to([128, 2, 4, 128]), ALU.add))
            p.wait("vector", DV_, t)
        for ci in range(2):
            ch = c2 + ci * 64
            cf = ci * 64
            sub = tb[cf:cf + 64, TB_SU + tcol:TB_SU + tcol + 8] \
                .unsqueeze(2).broadcast_to([64, 8, 128])
            sqb = tb[cf:cf + 64, TB_SQ + tcol:TB_SQ + tcol + 8] \
                .unsqueeze(2).broadcast_to([64, 8, 128])
            skb = tb[cf:cf + 64, TB_SK + tcol:TB_SK + tcol + 8] \
                .rearrange("p (a r) -> p a r", r=2) \
                .unsqueeze(3).broadcast_to([64, 4, 2, 128])
            p.wait("tensor", DV_, p.cnt[DV_])
            for vh in range(8):
                p.op("tensor", lambda e, vh=vh, ch=ch, cf=cf: e.matmul(
                    ab.ps1k[2][cf:cf + 64, vh * 128:vh * 128 + 128],
                    ab.k_dm[:, (vh // 2) * 1024 + ch:(vh // 2) * 1024 + ch + 64],
                    ab.state_bf[:, vh * 128:vh * 128 + 128],
                    start=True, stop=True), sem=DT if vh == 7 else None)
            mt_upre = p.cnt[DT]
            p.wait("vector", DT, mt_upre)
            t = _fv(p, lambda e, cf=cf, sub=sub: e.tensor_tensor(
                v3(ab.tsb[cf:cf + 64, 0:1024]), v3(ab.ps1k[2][cf:cf + 64, :]),
                sub, ALU.mult))
            p.wait("vector", DV_, t)
            t_u0 = _fv(p, lambda e, cf=cf: e.tensor_tensor(
                ab.u[0][cf:cf + 64, :], ab.vbc[cf:cf + 64, :],
                ab.tsb[cf:cf + 64, 0:1024], ALU.subtract))
            ucur = 0
            p.wait("tensor", DV_, t_u0)
            for lvl in range(4):
                for vh in range(8):
                    p.op("tensor", lambda e, vh=vh, cf=cf, lvl=lvl, ucur=ucur:
                         e.matmul(
                             ab.ps1k[3][cf:cf + 64, vh * 128:vh * 128 + 128],
                             ab.A[lvl][cf:cf + 64,
                                       vh * 128 + cf:vh * 128 + cf + 64],
                             ab.u[ucur][cf:cf + 64, vh * 128:vh * 128 + 128],
                             start=True, stop=True), sem=DT if vh == 7 else None)
                mt_app = p.cnt[DT]
                p.wait("scalar", DT, mt_app)
                dst = ab.vnew if lvl == 3 else ab.u[1 - ucur]
                t_ev = p.op("scalar", lambda e, cf=cf, dst=dst: e.copy(
                    dst[cf:cf + 64, :], ab.ps1k[3][cf:cf + 64, :]), sem=DS)
                p.wait("tensor", DS, t_ev)
                ucur = 1 - ucur
            for vh in range(8):
                p.op("tensor", lambda e, vh=vh, ch=ch, cf=cf: e.matmul(
                    ab.ps1k[2][cf:cf + 64, vh * 128:vh * 128 + 128],
                    ab.q_dm[:, (vh // 2) * 1024 + ch:(vh // 2) * 1024 + ch + 64],
                    ab.state_bf[:, vh * 128:vh * 128 + 128],
                    start=True, stop=True), sem=DT if vh == 7 else None)
            mt_o1 = p.cnt[DT]
            for vh in range(8):
                p.op("tensor", lambda e, vh=vh, cf=cf: e.matmul(
                    ab.ps1k[3][cf:cf + 64, vh * 128:vh * 128 + 128],
                    ab.aT[cf:cf + 64, vh * 128 + cf:vh * 128 + cf + 64],
                    ab.vnew[cf:cf + 64, vh * 128:vh * 128 + 128],
                    start=True, stop=True), sem=DT if vh == 7 else None)
            mt_o2 = p.cnt[DT]
            p.wait("vector", DT, mt_o1)
            t = _fv(p, lambda e, cf=cf, sqb=sqb: e.tensor_tensor(
                v3(ab.tsb[cf:cf + 64, 0:1024]), v3(ab.ps1k[2][cf:cf + 64, :]),
                sqb, ALU.mult))
            p.wait("vector", DT, mt_o2)
            p.wait("vector", DV_, t)
            t_out = _fv(p, lambda e, cf=cf, pr=pr: e.tensor_tensor(
                ab.outb[cf:cf + 64, pr * 1024:pr * 1024 + 1024],
                ab.tsb[cf:cf + 64, 0:1024], ab.ps1k[3][cf:cf + 64, :], ALU.add))
            krep = ab.kTM[cf:cf + 64, :].rearrange("p (a b) -> p a b", b=128) \
                .unsqueeze(2).broadcast_to([64, 4, 2, 128])
            t = _fv(p, lambda e, cf=cf, krep=krep, skb=skb: e.tensor_tensor(
                v4(ab.kmod[cf:cf + 64, :]), krep, skb, ALU.mult))
            p.wait("tensor", DV_, t)
            for vh in range(8):
                p.op("tensor", lambda e, vh=vh, cf=cf: e.matmul(
                    ab.ps1k[2][:, vh * 128:vh * 128 + 128],
                    ab.kmod[cf:cf + 64, vh * 128:vh * 128 + 128],
                    ab.vnew[cf:cf + 64, vh * 128:vh * 128 + 128],
                    start=True, stop=True), sem=DT if vh == 7 else None)
            mt_su = p.cnt[DT]
            t_eb = p.op("tensor", lambda e, cf=cf, tcol=tcol: e.matmul(
                ab.ps[7][:, 0:8], sb.onesb[cf:cf + 1, 0:128],
                tb[cf:cf + 1, TB_ELB + tcol:TB_ELB + tcol + 8],
                start=True, stop=True), sem=DT)
            p.wait("vector", DT, t_eb)
            t = _fv(p, lambda e: e.tensor_copy(ab.elb[:], ab.ps[7][:, 0:8]))
            p.wait("vector", DV_, t)
            ebv = ab.elb[:].unsqueeze(2).broadcast_to([128, 8, 128])
            t = _fv(p, lambda e, ebv=ebv: e.tensor_tensor(
                v3(ab.tsb[:]), v3(ab.state[:]), ebv, ALU.mult))
            p.wait("vector", DT, mt_su)
            p.wait("vector", DV_, t)
            t = _fv(p, lambda e: e.tensor_tensor(
                ab.state[:], ab.tsb[:], ab.ps1k[2][:], ALU.add))
            p.wait("vector", DV_, t)
            _fv(p, lambda e: e.tensor_copy(ab.state_bf[:], ab.state[:]))
    return p.cnt[DV_]


def emit_attn_epilogue(p, ab, sb, aps, hf):
    DV_, DS = "dl_v", "dl_s"
    p.sem("ep_st")
    if "ap_s" in p.cnt:
        p.wait("vector", "ap_s", p.cnt["ap_s"])
    nwv = ab.nwbc[:].unsqueeze(1).broadcast_to([128, 8, 128])

    def r3(x):
        return x.rearrange("p (a b) -> p a b", b=128)

    for pr in range(8):
        t = _fv(p, lambda e, pr=pr: e.tensor_tensor(
            ab.tsb[:], ab.outb[:, pr * 1024:pr * 1024 + 1024],
            ab.z_act[:, pr * 1024:pr * 1024 + 1024], ALU.mult))
        p.wait("scalar", DV_, t)
        t = p.op("scalar", lambda e: e.activation(
            ab.rec[:, 0:1024], ab.tsb[:], AF.Square), sem=DS)
        p.wait("vector", DS, t)
        _fv(p, lambda e, pr=pr: e.reduce_sum(
            ab.epiv[:, pr * 8:pr * 8 + 8], r3(ab.rec[:, 0:1024]),
            axis=AX.X))
    sbb = sb.sbb
    p.wait("vector", DV_, p.cnt[DV_])
    _vc(p, lambda e: e.tensor_scalar(sbb[:, 0:64], ab.epiv[:], 1.0 / 128.0,
                                     None, ALU.mult))
    _vc(p, lambda e: e.tensor_scalar(sbb[:, 0:64], sbb[:, 0:64], float(RMS_EPS),
                                     None, ALU.add))
    I32 = mybir.dt.int32
    _vc(p, lambda e: e.tensor_scalar(
        sbb[:, 64:128].bitcast(I32), sbb[:, 0:64].bitcast(I32), 1, None,
        ALU.logical_shift_right))
    _vc(p, lambda e: e.tensor_scalar(
        sbb[:, 64:128].bitcast(I32), sbb[:, 64:128].bitcast(I32), 0, None,
        ALU.bitwise_not))
    _vc(p, lambda e: e.tensor_scalar(
        ab.epiv[:].bitcast(I32), sbb[:, 64:128].bitcast(I32),
        0x5F3759DF + 1, None, ALU.add))
    for _ in range(3):
        _vc(p, lambda e: e.tensor_tensor(sbb[:, 64:128], ab.epiv[:], ab.epiv[:],
                                         ALU.mult))
        _vc(p, lambda e: e.tensor_tensor(sbb[:, 64:128], sbb[:, 64:128],
                                         sbb[:, 0:64], ALU.mult))
        _vc(p, lambda e: e.tensor_scalar(sbb[:, 64:128], sbb[:, 64:128],
                                         -0.5, 1.5, ALU.mult, ALU.add))
        _vc(p, lambda e: e.tensor_tensor(ab.epiv[:], ab.epiv[:], sbb[:, 64:128],
                                         ALU.mult))
    for pr in range(8):
        t = _fv(p, lambda e, pr=pr: e.tensor_tensor(
            ab.tsb[:], ab.outb[:, pr * 1024:pr * 1024 + 1024],
            ab.z_act[:, pr * 1024:pr * 1024 + 1024], ALU.mult))
        rsv = ab.epiv[:, pr * 8:pr * 8 + 8].unsqueeze(2)             .broadcast_to([128, 8, 128])
        p.wait("vector", DV_, t)
        t = _fv(p, lambda e, rsv=rsv: e.tensor_tensor(
            r3(ab.rec[:, 0:1024]), r3(ab.tsb[:]), rsv, ALU.mult))
        if p.cnt["ep_st"] >= 16:
            p.wait("vector", "ep_st", p.cnt["ep_st"])
        p.wait("vector", DV_, t)
        t = _fv(p, lambda e, nwv=nwv: e.tensor_tensor(
            r3(ab.u[0][:]), r3(ab.rec[:, 0:1024]), nwv, ALU.mult))
        p.wait("sync", DV_, t)
        p.dma("sync",
              aps["i2b"][hf * 1024 + pr * 128:hf * 1024 + pr * 128 + 128, :],
              ab.u[0][:], sem="ep_st")
    return p.cnt["ep_st"]


def emit_attn(p, ab, sb, aps, nhalves=8, stages="APDE"):
    p.sem("xf_ld")
    p.sem("ap_s")
    p.sem("ep_st")
    if p.cnt.get("ac_v", 0) > 0:
        p.wait("vector", "ac_v", p.cnt["ac_v"])
    t0 = p.op("vector", lambda e: e.memset(ab.state[:], 0.0), sem="dl_v")
    t0 = p.op("vector", lambda e: e.memset(ab.state_bf[:], 0.0), sem="dl_v")
    t0 = p.op("vector", lambda e: e.memset(ab.raw[:], 0.0), sem="ap_v")
    emit_attn_precast(p, ab, aps["ins"], aps)
    for hf in range(nhalves):
        if p.cnt.get("ap_t", 0) > 0:
            p.wait("sync", "ap_t", p.cnt["ap_t"])
        for sn in ("dl_t", "dl_v", "dl_s"):
            if p.cnt.get(sn, 0) > 0:
                p.wait("sync", sn, p.cnt[sn])
        if p.cnt.get("ep_st", 0) > 0:
            p.wait("sync", "ep_st", p.cnt["ep_st"])
        xf = None
        for hs in range(16):
            xf = p.dma("sync", ab.x_fm[:, hs * 1024:hs * 1024 + 1024],
                       aps["x_gath"][hf * 1024:hf * 1024 + 1024,
                                     hs * 128:hs * 128 + 128],
                       sem="xf_ld", transpose=True)
        if "A" in stages:
            emit_attn_tables(p, ab, sb, hf, xf)
        if "P" in stages:
            emit_attn_proj(p, ab, sb, aps, hf)
        if "D" in stages:
            emit_attn_delta(p, ab, sb, hf)
        if "E" in stages:
            emit_attn_epilogue(p, ab, sb, aps, hf)
    return p.cnt["ep_st"]



class WoutBufs:

    def __init__(self, p, ar, pb):
        ar.reset()
        self.wo = ar.bf16(32 * 2048)
        self.afm = ar.bf16(32 * 512)
        self.xs = [ar.f32(2048) for _ in range(2)]
        self.ev = [ar.f32(2048) for _ in range(2)]
        self.ps1k = pb.ps1k


def emit_wout(p, wb, sb, aps, slice_off):
    for s in ("wo_t", "wo_v"):
        p.sem(s)
    for eng in ("sync", "gpsimd", "tensor", "vector", "scalar"):
        for sname in ("dl_t", "dl_v", "dl_s", "ap_s", "ap_v", "ep_st"):
            if p.cnt.get(sname, 0) > 0:
                p.wait(eng, sname, p.cnt[sname])
    if p.cnt.get("cc", 0) > 0:
        p.wait("sync", "cc", p.cnt["cc"])
        p.wait("gpsimd", "cc", p.cnt["cc"])
    for ch in range(8):
        par = ch % 2
        p.dma("gpsimd",
              wb.wo[:, ch * 8192:ch * 8192 + 8192]
              .rearrange("p (a b) -> p a b", a=4),
              aps["W_out"].rearrange("(a p) i -> p a i", p=128)
              [:, ch * 4:ch * 4 + 4, :],
              sem=f"wo_ld{par}")
    for par in range(2):
        p.wait("tensor", f"wo_ld{par}", p.cnt[f"wo_ld{par}"])
    for q in range(4):
        if q > 0:
            p.wait("sync", "wo_t", p.cnt["wo_t"])
        af = None
        for fc in range(32):
            g, c0 = fc // 8, (fc % 8) * 128
            r0 = g * 8192 + slice_off + q * 512
            af = p.dma("sync", wb.afm[:, fc * 512:fc * 512 + 512],
                       aps["o2b"][r0:r0 + 512, c0:c0 + 128],
                       sem="af_ld", transpose=True)
        p.wait("tensor", "af_ld", af)
        for t in range(4):
            j = t % 2
            tok0 = q * 512 + t * 128
            if p.cnt["wo_v"] >= 2:
                p.wait("sync", "wo_v", p.cnt["wo_v"] - 2)
            xl = p.dma("sync", wb.xs[j][:], aps["x1"][tok0:tok0 + 128, :],
                       sem=f"wo_xs{j}")
            if p.cnt["wo_v"] >= 2:
                p.wait("tensor", "wo_v", p.cnt["wo_v"] - 2)
            for fc in range(32):
                for dq in range(4):
                    p.op("tensor", lambda e, fc=fc, dq=dq, j=j, t=t: e.matmul(
                        wb.ps1k[2 * j + dq // 2][:, (dq % 2) * 512:
                                                 (dq % 2) * 512 + 512],
                        wb.afm[:, fc * 512 + t * 128:fc * 512 + t * 128 + 128],
                        wb.wo[:, fc * 2048 + dq * 512:fc * 2048 + dq * 512 + 512],
                        start=(fc == 0), stop=(fc == 31)),
                        sem="wo_t" if (fc == 31 and dq == 3) else None)
            mt = p.cnt["wo_t"]
            p.wait("vector", "wo_t", mt)
            p.wait("vector", f"wo_xs{j}", xl)
            if p.cnt.get(f"wo_st{j}", 0) > 0:
                p.wait("vector", f"wo_st{j}", p.cnt[f"wo_st{j}"])
            for hh in range(2):
                p.op("vector", lambda e, j=j, hh=hh: e.tensor_tensor(
                    wb.ev[j][:, hh * 1024:hh * 1024 + 1024],
                    wb.ps1k[2 * j + hh][:],
                    wb.xs[j][:, hh * 1024:hh * 1024 + 1024], ALU.add),
                    sem="wo_v")
            p.wait("sync", "wo_v", p.cnt["wo_v"])
            p.dma("sync", aps["x2"][tok0:tok0 + 128, :], wb.ev[j][:],
                  sem=f"wo_st{j}")
    return p.cnt.get("wo_st0", 0), p.cnt.get("wo_st1", 0)


def host_wout(core, W_out):
    kh0 = (core % 4) * 4
    vhs = [2 * k + r for k in range(kh0, kh0 + 4) for r in range(2)]
    rows = [vh * 128 + d for vh in vhs for d in range(128)]
    return dict(Wo_c=np.ascontiguousarray(W_out[rows, :]))


class WoutPBufs:

    def __init__(self, p, ar, pb):
        ar.reset()
        self.wo = ar.bf16(8 * 2048)
        self.afm = [ar.bf16(8 * 512) for _ in range(2)]
        self.ev = [ar.bf16(2048) for _ in range(2)]
        self.xs = [ar.f32(2048) for _ in range(2)]
        self.rb = [ar.bf16(2048) for _ in range(2)]
        self.x2o = [ar.f32(2048) for _ in range(2)]
        self.ps1k = pb.ps1k


def emit_wout_partial(p, wb, sb, aps):
    for s in ("wo_t", "wo_s"):
        p.sem(s)
    for eng in ("sync", "gpsimd", "tensor", "vector", "scalar"):
        for sname in ("dl_t", "dl_v", "dl_s", "ap_s", "ap_v", "ep_st"):
            if p.cnt.get(sname, 0) > 0:
                p.wait(eng, sname, p.cnt[sname])
    lw = p.dma("gpsimd", wb.wo[:].rearrange("p (a b) -> p a b", a=8),
               aps["Wo_c"].rearrange("(a p) i -> p a i", p=128), sem="wo_ld")
    p.wait("tensor", "wo_ld", lw)
    for tg in range(16):
        par = tg % 2
        if tg >= 2:
            p.wait("sync", "wo_t", p.cnt["wo_t"] - 4)
        af = None
        for fc in range(8):
            r0 = tg * 512
            af = p.dma("sync", wb.afm[par][:, fc * 512:fc * 512 + 512],
                       aps["i2b"][r0:r0 + 512, fc * 128:fc * 128 + 128],
                       sem=f"af_ld{par}", transpose=True)
        p.wait("tensor", f"af_ld{par}", af)
        for t in range(4):
            gt = tg * 4 + t
            j = gt % 2
            if p.cnt["wo_s"] >= 4:
                p.wait("tensor", "wo_s", p.cnt["wo_s"] - 2)
            for fc in range(8):
                for dq in range(4):
                    p.op("tensor", lambda e, fc=fc, dq=dq, j=j, t=t, par=par:
                         e.matmul(
                             wb.ps1k[2 * j + dq // 2][:, (dq % 2) * 512:
                                                      (dq % 2) * 512 + 512],
                             wb.afm[par][:, fc * 512 + t * 128:
                                         fc * 512 + t * 128 + 128],
                             wb.wo[:, fc * 2048 + dq * 512:
                                   fc * 2048 + dq * 512 + 512],
                             start=(fc == 0), stop=(fc == 7)),
                         sem="wo_t" if (fc == 7 and dq == 3) else None)
            mt = p.cnt["wo_t"]
            p.wait("scalar", "wo_t", mt)
            if p.cnt.get(f"wo_st{j}", 0) > 0:
                p.wait("scalar", f"wo_st{j}", p.cnt[f"wo_st{j}"])
            for hh in range(2):
                p.op("scalar", lambda e, j=j, hh=hh: e.copy(
                    wb.ev[j][:, hh * 1024:hh * 1024 + 1024],
                    wb.ps1k[2 * j + hh][:]), sem="wo_s")
            p.wait("sync", "wo_s", p.cnt["wo_s"])
            p.dma("sync", aps["pt"][gt * 128:gt * 128 + 128, :], wb.ev[j][:],
                  sem=f"wo_st{j}")
    return p.cnt.get("wo_st0", 0), p.cnt.get("wo_st1", 0)


def emit_x2add(p, wb, aps):
    p.wait("sync", "cc", p.cnt["cc"])
    for t in range(16):
        par = t % 2
        if p.cnt.get(f"x2_st{par}", 0) > 0:
            p.wait("sync", f"x2_st{par}", p.cnt[f"x2_st{par}"])
        l1 = p.dma("sync", wb.xs[par][:], aps["x1"][t * 128:t * 128 + 128, :],
                   sem=f"x2_l{par}")
        l2 = p.dma("sync", wb.rb[par][:], aps["rsb"][t * 128:t * 128 + 128, :],
                   sem=f"x2_r{par}")
        p.wait("vector", f"x2_l{par}", l1)
        p.wait("vector", f"x2_r{par}", l2)
        tv = p.op("vector", lambda e, par=par: e.tensor_tensor(
            wb.x2o[par][:], wb.xs[par][:], wb.rb[par][:], ALU.add), sem="x2_v")
        p.wait("sync", "x2_v", tv)
        p.dma("sync", aps["x2"][t * 128:t * 128 + 128, :], wb.x2o[par][:],
              sem=f"x2_st{par}")
    return p.cnt.get("x2_st0", 0), p.cnt.get("x2_st1", 0)


def emit_era_gate(p, sems=None):
    names = [s for s in (sems or list(p.cnt)) if p.cnt.get(s, 0) > 0]
    for eng in ENGINES:
        for s in names:
            p.wait(eng, s, p.cnt[s])



from contextlib import ExitStack
from concourse.bass_utils import run_bass_kernel_spmd


def _build():
    nc = bass.Bass("TRN2", target_bir_lowering=False)
    ins = {}

    def di(n, shp, dt=F32):
        ins[n] = nc.dram_tensor(n, shp, dt, kind="ExternalInput").ap()

    di("x", [TOK, D])
    di("Wg_s", [NIC, D, 128]); di("Wu_s", [NIC, D, 128])
    di("Wd_s", [4, NIC, 128, 512])
    di("gs2", [128, NIC]); di("us2", [128, NIC]); di("ds_row", [1, D])
    di("Wqkv_s", [16, D, 128]); di("Wz_s", [16, 128, 1024])
    di("Wba_c", [D, 16]); di("convw", [16, 128, 4])
    di("A_c", [1, 8]); di("dt_c", [1, 8]); di("norm_weight", [128])
    di("Wo_c", [1024, D])
    di("SEL", [16, 16, 128]); di("SEL64", [128, 128]); di("U1BD", [128, 128])
    di("MLOW4", [128, 512], BF16); di("MUPS4", [128, 512], BF16)
    di("MUPI4", [128, 512], BF16)
    di("IDENT", [128, 128], BF16); di("IDENTF", [128, 128]); di("ONESF", [1, 128])
    out = nc.dram_tensor("out", [TOK, D], F32, kind="ExternalOutput").ap()
    x1_t = nc.dram_tensor("x1", [TOK, D], F32)
    x2_t = nc.dram_tensor("x2", [TOK, D], F32)
    i1b_t = nc.dram_tensor("i1b", [TOK, D], BF16)
    xg_t = nc.dram_tensor("x_gath", [8192, D], BF16)
    i2b_t = nc.dram_tensor("i2b", [8192, 1024], BF16)
    pt_t = nc.dram_tensor("pt", [8192, D], BF16)
    rsb_t = nc.dram_tensor("rsb", [TOK, D], BF16)
    wqbf = nc.dram_tensor("Wqkv_bf", [16, D, 128], BF16)
    wzbf = nc.dram_tensor("Wz_bf", [16, 128, 1024], BF16)
    RG = [[0, 1, 2, 3], [4, 5, 6, 7]]
    with ExitStack() as ctx:
        p = Prog(nc, ctx)
        sb = SharedBufs(p)
        ar = Arena(p)
        pb = PsumBufs(p)
        fb = FFNBufs(p, ar, pb)
        sb.load_consts(p, ins)
        emit_ffn_consts(p, fb, sb, ins)
        emit_ffn_pass(p, fb, sb, ins["x"], x1_t.ap(), ins, "f1")
        p.wait("sync", "ffn_stx0", p.cnt["ffn_stx0"])
        p.wait("sync", "ffn_stx1", p.cnt["ffn_stx1"])
        emit_ln_store_pass(p, sb, fb, x1_t.ap(), i1b_t.ap())
        p.wait("gpsimd", "lnp_st0", p.cnt["lnp_st0"])
        p.wait("gpsimd", "lnp_st1", p.cnt["lnp_st1"])
        p.op("gpsimd", lambda e: e.collective_compute(
            "AllGather", mybir.AluOpType.bypass, replica_groups=RG,
            ins=[i1b_t.ap().opt()], outs=[xg_t.ap().opt()]), sem="cc")
        emit_era_gate(p)
        ab = AttnBufs(p, ar, pb)
        emit_attn_consts(p, ab, sb, ins)
        aps_a = dict(x_gath=xg_t.ap(), i2b=i2b_t.ap(), Wqkv_bf=wqbf.ap(),
                     Wz_bf=wzbf.ap(), ins=ins)
        p.wait("sync", "cc", 1)
        emit_attn(p, ab, sb, aps_a, nhalves=8, stages="APDE")
        wb = WoutPBufs(p, ar, pb)
        aps_w = dict(Wo_c=ins["Wo_c"], i2b=i2b_t.ap(), pt=pt_t.ap(),
                     x1=x1_t.ap(), rsb=rsb_t.ap(), x2=x2_t.ap())
        s0, s1 = emit_wout_partial(p, wb, sb, aps_w)
        p.wait("gpsimd", "wo_st0", s0)
        p.wait("gpsimd", "wo_st1", s1)
        p.op("gpsimd", lambda e: e.collective_compute(
            "ReduceScatter", mybir.AluOpType.add, replica_groups=RG,
            ins=[pt_t.ap().opt()], outs=[rsb_t.ap().opt()]), sem="cc")
        emit_x2add(p, wb, aps_w)
        emit_era_gate(p)
        emit_ffn_consts(p, fb, sb, ins)
        p.wait("tensor", "v_dsb", p.cnt["v_dsb"])
        emit_ffn_pass(p, fb, sb, x2_t.ap(), out, ins, "f2")
        p.wait("sync", "ffn_stx0", p.cnt["ffn_stx0"])
        p.wait("sync", "ffn_stx1", p.cnt["ffn_stx1"])
        p.run_block()
    return nc


_BUILT = None


def kernel(**inputs):
    global _BUILT
    if _BUILT is None:
        _BUILT = _build()
    nc = _BUILT
    hs = np.asarray(inputs["L0_attn_hidden_states"], dtype=np.float32)
    W = host_ffn_weights(
        np.asarray(inputs["L0_ffn_gate_w"], np.float32),
        np.asarray(inputs["L0_ffn_up_w"], np.float32),
        np.asarray(inputs["L0_ffn_down_w"], np.float32),
        np.asarray(inputs["L0_ffn_gate_s"], np.float32),
        np.asarray(inputs["L0_ffn_up_s"], np.float32),
        np.asarray(inputs["L0_ffn_down_s"], np.float32))
    C = host_consts()
    AC = host_attn_consts()
    nw = np.ascontiguousarray(np.asarray(inputs["L0_attn_norm_weight"],
                                         np.float32))
    in_maps = []
    for c in range(8):
        b, s = c // 4, c % 4
        A = host_attn_weights(
            c, np.asarray(inputs["L0_attn_W_qkvz"], np.float32),
            np.asarray(inputs["L0_attn_W_ba"], np.float32),
            np.asarray(inputs["L0_attn_conv_kernel"], np.float32),
            np.asarray(inputs["L0_attn_A_log"], np.float32),
            np.asarray(inputs["L0_attn_dt_bias"], np.float32))
        WO = host_wout(c, np.asarray(inputs["L0_attn_W_out"], np.float32))
        in_maps.append(dict(
            x=np.ascontiguousarray(hs[b, s * TOK:(s + 1) * TOK, :]),
            norm_weight=nw, **W, **A, **WO, **C, **AC))
    res = run_bass_kernel_spmd(nc, in_maps, core_ids=list(range(8)))
    full = np.empty((2, 8192, D), np.float32)
    for c in range(8):
        b, s = c // 4, c % 4
        full[b, s * TOK:(s + 1) * TOK, :] = res.results[c]["out"]
    return full
